# version 04
# baseline (speedup 1.0000x reference)
import numpy as np
import ml_dtypes
import concourse.bass as bass
import concourse.mybir as mybir
from concourse.bass_utils import run_bass_kernel_spmd

F32 = mybir.dt.float32
BF16 = mybir.dt.bfloat16
AF = mybir.ActivationFunctionType
ALU = mybir.AluOpType

NCORES = 8
D = 1024
T = 2048
NST = 16
DTR = 64
DCH = 8
HID = 4096
MH = HID // 128
SLAB = 512
NSL = T // SLAB
BIG = 1.0e30
DT_MIN, DT_MAX = 0.001, 0.1


class Prog:
    def __init__(self, nc):
        self.nc = nc
        self.events = []
        self.cnt = {}
        self.sems = {}
        self.guards = []

    def mksem(self, name):
        g = self.nc.semaphore(name)
        h = g.__enter__()
        self.guards.append(g)
        self.sems[name] = h
        self.cnt[name] = 0

    def step(self, eng, emit, deps=(), inc=1, sem=None):
        sname = sem or eng
        waits = [(self.sems[s], v) for (s, v) in deps if v > 0]
        if inc:
            self.cnt[sname] += inc
        after = self.cnt[sname]
        smh = self.sems[sname]

        def fn(e, waits=waits, inc=inc, smh=smh, emit=emit):
            for h, v in waits:
                e.wait_ge(h, v)
            inst = emit(e)
            if inc:
                inst.then_inc(smh, inc)

        self.events.append((eng, fn))
        return (sname, after)

    def dma(self, eng, transfers, deps=(), sem=None):
        if sem is None:
            pool = self.dma_pools[eng]
            sname = pool[self.dma_rr[eng] % len(pool)]
            self.dma_rr[eng] += 1
        else:
            sname = sem
        waits = [(self.sems[s], v) for (s, v) in deps if v > 0]
        prev_total = self.cnt[sname]
        self.cnt[sname] += 16 * len(transfers)
        after = self.cnt[sname]
        smh = self.sems[sname]

        def fn(e, waits=waits, transfers=transfers, smh=smh, prev=prev_total):
            if prev > 0:
                e.wait_ge(smh, prev)
            for h, v in waits:
                e.wait_ge(h, v)
            for out, in_ in transfers:
                e.dma_start(out=out, in_=in_).then_inc(smh, 16)

        self.events.append((eng, fn))
        return (sname, after)

    def wait_only(self, eng, deps):
        waits = [(self.sems[s], v) for (s, v) in deps if v > 0]

        def fn(e, waits=waits):
            for h, v in waits:
                e.wait_ge(h, v)

        self.events.append((eng, fn))

    def emit(self, block):
        evs = self.events

        @block.sync
        def _(e):
            for k, fn in evs:
                if k == "sp":
                    fn(e)

        @block.tensor
        def _(e):
            for k, fn in evs:
                if k == "pe":
                    fn(e)

        @block.scalar
        def _(e):
            for k, fn in evs:
                if k == "act":
                    fn(e)

        @block.vector
        def _(e):
            for k, fn in evs:
                if k == "dve":
                    fn(e)

        @block.gpsimd
        def _(e):
            for k, fn in evs:
                if k == "pool":
                    fn(e)


def build_program(stage=4):
    full = stage >= 4
    nc = bass.Bass("TRN2", target_bir_lowering=False, detect_race_conditions=False)

    def din(name, shape, dt=F32):
        return nc.dram_tensor(name, list(shape), dt, kind="ExternalInput").ap()

    x_h = din("x", [D, T])
    w_in_h = din("w_in", [D, D])
    w_x_h = din("w_x", [D, 96])
    w_dt_h = din("w_dt", [DTR, D])
    bdt_h = din("b_dt", [128, DCH])
    Alog_h = din("A_log", [D, NST])
    Dp_h = din("Dp", [128, DCH])
    w1m_h = din("w1m", [32, 128, 8, 128], BF16)
    w2m_h = din("w2m", [32, 128, 8, 128], BF16)
    h1m_h = din("h1m", [32, 128, 32, 128], BF16)
    h2m_h = din("h2m", [32, 128, 32, 128], BF16)
    w3m_h = din("w3m", [8, 128, 32, 128], BF16)
    s1_h = din("s1", [128, 32])
    s2_h = din("s2", [128, 32])
    sh1_h = din("sh1", [128, 32])
    sh2_h = din("sh2", [128, 32])
    s3_h = din("s3", [128, DCH])
    m2b_h = din("m2b", [128, 72])
    ones_h = din("onesc", [128, 1])
    cns_h = din("cns", [128, 4])

    out_h = nc.dram_tensor("out", [D, T], F32, kind="ExternalOutput").ap()
    dbg = {}
    if stage == 5:
        dbg["dbg_xmid"] = nc.dram_tensor("dbg_xmid", [D, T], F32, kind="ExternalOutput").ap()
    if stage < 4:
        def dout(name, shape, dt=F32):
            dbg[name] = nc.dram_tensor(name, list(shape), dt, kind="ExternalOutput").ap()
            return dbg[name]
        dout("dbg_dt", [128, DCH * T], BF16)
        dout("dbg_u", [128, DCH * SLAB], BF16)
        dout("dbg_g", [NST, T])
        dout("dbg_c", [NST, T])
        if stage >= 2:
            dout("dbg_hb", [128, 128])
            dout("dbg_hin", [128, 128])
        if stage >= 3:
            dout("dbg_xmid", [D, T])

    g_dram = nc.dram_tensor("g_scr", [NST, T], F32).ap()
    c_dram = nc.dram_tensor("c_scr", [NST, T], F32).ap()
    y0_dram = nc.dram_tensor("y0_scr", [D, T], F32).ap()
    snd_d = nc.dram_tensor("snd_scr", [128, 136], F32)
    rcv_d = nc.dram_tensor("rcv_scr", [128 * NCORES, 136], F32)
    xmid_d = nc.dram_tensor("xmid_scr", [D, T], F32).ap()
    lnrow_d = nc.dram_tensor("lnrow_scr", [2, SLAB], F32).ap()

    p = Prog(nc)
    for s in ["pe", "act", "dve", "pool", "sp", "cc"]:
        p.mksem(s)
    p.dma_pools = {"sp": [], "pool": []}
    p.dma_rr = {"sp": 0, "pool": 0}
    for i in range(6):
        nm = "spq%d" % i
        p.mksem(nm)
        p.dma_pools["sp"].append(nm)
    for i in range(8):
        nm = "plq%d" % i
        p.mksem(nm)
        p.dma_pools["pool"].append(nm)

    live = {}

    def alloc(name, shape, dt, side="left"):
        g = nc.sbuf_tensor(name, list(shape), dt, side=side)
        t = g.__enter__()
        live[name] = g
        return t

    def free(*names):
        for n in names:
            live.pop(n).__exit__(None, None, None)

    palloc_guards = {}

    def palloc(name, shape, dt=F32):
        g = nc.psum_tensor(name, list(shape), dt)
        t = g.__enter__()
        palloc_guards[name] = g
        return t

    def pfree(*names):
        for n in names:
            palloc_guards.pop(n).__exit__(None, None, None)

    A_sc = alloc("A_sc", [128, 128], F32)
    A_bf = alloc("A_bf", [128, 128], BF16)
    bdt_sc = alloc("bdt_sc", [128, DCH], F32)
    Dp_sc = alloc("Dp_sc", [128, DCH], F32)
    s1_sc = alloc("s1_sc", [128, 32], F32)
    s2_sc = alloc("s2_sc", [128, 32], F32)
    sh1_sc = alloc("sh1_sc", [128, 32], F32)
    sh2_sc = alloc("sh2_sc", [128, 32], F32)
    s3_sc = alloc("s3_sc", [128, DCH], F32)
    m2b_sb = alloc("m2b_sb", [128, 72], F32)
    ones_f = alloc("ones_f", [128, 1], F32)
    ones_b = alloc("ones_b", [128, 1], BF16)
    cns = alloc("cns_sb", [128, 4], F32)
    ones8 = alloc("ones8", [128, 8], F32)
    Hb = alloc("Hb", [128, 128], F32)
    hin = alloc("hin", [128, 128], F32)
    Sdt_sb = alloc("Sdt_sb", [128, DCH], F32)

    wcur = [alloc("wcur0", [128, 4096], BF16, side="right"),
            alloc("wcur1", [128, 4096], BF16, side="right")]
    Xf2 = [alloc("Xf0", [128, DCH * SLAB], BF16, side="right"),
           alloc("Xf1", [128, DCH * SLAB], BF16, side="right")]

    dt_all = alloc("dt_all", [128, DCH * T], BF16)

    e_cl = p.dma("sp", [
        (bdt_sc[:], bdt_h[:]), (Dp_sc[:], Dp_h[:]),
        (s1_sc[:], s1_h[:]), (s2_sc[:], s2_h[:]),
        (sh1_sc[:], sh1_h[:]), (sh2_sc[:], sh2_h[:]), (s3_sc[:], s3_h[:]),
        (m2b_sb[:], m2b_h[:]), (ones_f[:], ones_h[:]), (cns[:], cns_h[:]),
    ] + [(A_sc[:, dc * 16:(dc + 1) * 16], Alog_h[dc * 128:(dc + 1) * 128, :])
         for dc in range(DCH)])
    e_A1 = p.step("act", lambda e: e.activation(A_sc[:], A_sc[:], AF.Exp), deps=[e_cl])
    e_A2 = p.step("dve", lambda e: e.tensor_scalar_mul(A_sc[:], A_sc[:], -1.0), deps=[e_A1])
    e_A3 = p.step("dve", lambda e: e.tensor_copy(A_bf[:], A_sc[:]), deps=[e_A2])
    e_ones = p.step("dve", lambda e: e.tensor_copy(ones_b[:], ones_f[:]), deps=[e_cl])
    e_ones8 = p.step("dve", lambda e: e.memset(ones8[:], 1.0), deps=[])

    w_in_bf = alloc("w_in_bf", [128, DCH * D], BF16)
    wx_bf = alloc("wx_bf", [128, DCH * 96], BF16)
    wdt_bf = alloc("wdt_bf", [DTR, D], BF16)
    xy = alloc("xy", [128, DCH * SLAB], F32)
    xb = alloc("xb", [128, DCH * SLAB], BF16)
    sq = alloc("sq", [128, DCH * SLAB], BF16)
    xin_bf = alloc("xin_bf", [128, DCH * SLAB], BF16)
    mu_r = alloc("mu_r", [1, SLAB], F32)
    vr_r = alloc("vr_r", [1, SLAB], F32)
    bw_r = alloc("bw_r", [1, SLAB], F32)
    arep = alloc("arep", [128, SLAB], F32)
    brep = alloc("brep", [128, SLAB], F32)
    dtl_bf = alloc("dtl_bf", [DTR, SLAB], BF16)
    B_sb = alloc("B_sb", [NST, SLAB], F32)
    g_sb = alloc("g_sb", [NST, SLAB], F32)
    z_t = alloc("z_t", [128, SLAB], F32)
    r_t = alloc("r_t", [128, SLAB], F32)
    wld = alloc("wld", [128, D], F32)

    ev_wcast = []
    prev_cast = (None, 0)
    for k in range(DCH):
        dep = [] if k == 0 else [prev_cast]
        e_l = p.dma("sp", [(wld[:], w_in_h[k * 128:(k + 1) * 128, :])], deps=dep)
        prev_cast = p.step("dve", lambda e, k=k: e.tensor_copy(
            w_in_bf[:, k * D:(k + 1) * D], wld[:]), deps=[e_l])
        ev_wcast.append(prev_cast)
    for k in range(DCH):
        e_l = p.dma("sp", [(wld[:, 0:96], w_x_h[k * 128:(k + 1) * 128, :])], deps=[prev_cast])
        prev_cast = p.step("dve", lambda e, k=k: e.tensor_copy(
            wx_bf[:, k * 96:(k + 1) * 96], wld[:, 0:96]), deps=[e_l])
    e_l = p.dma("sp", [(wld[0:DTR, :], w_dt_h[:])], deps=[prev_cast])
    e_wdt = p.step("dve", lambda e: e.tensor_copy(wdt_bf[:], wld[0:DTR, :]), deps=[e_l])
    ev_wready = e_wdt

    ps_su = palloc("ps_su", [1, SLAB])
    ps_sq = palloc("ps_sq", [1, SLAB])
    ps_a = palloc("ps_a", [128, SLAB])
    ps_b = palloc("ps_b", [128, SLAB])
    ps_dl = palloc("ps_dl", [DTR, SLAB])
    ps_B = palloc("ps_B", [NST, SLAB])
    ps_C = palloc("ps_C", [NST, SLAB])
    ps_g = palloc("ps_g", [NST, SLAB])

    e_dt_done = []
    e_g_done = []
    e_y0_list = []
    e_prev_y0 = (None, 0)
    last_xin_evac = [(None, 0), (None, 0)]
    fence = []

    for s in range(NSL):
        c0 = s * SLAB

        e_x = p.dma("pool", [(xy[:, dc * SLAB:(dc + 1) * SLAB],
                              x_h[dc * 128:(dc + 1) * 128, c0:c0 + SLAB])
                             for dc in range(DCH)], deps=[e_prev_y0])
        e_cast = p.step("dve", lambda e: e.tensor_copy(xb[:], xy[:]),
                        deps=[e_x, e_ones] + fence)
        e_sqr = p.step("act", lambda e: e.activation(sq[:], xy[:], AF.Square),
                       deps=[e_x, e_A1] + fence)
        e_st1 = p.step("pe", lambda e: [
            e.matmul(ps_su[:], ones_b[:, 0:1], xb[:, dc * SLAB:(dc + 1) * SLAB],
                     start=(dc == 0), stop=(dc == DCH - 1))
            for dc in range(DCH)][-1], deps=[e_cast])
        e_st2 = p.step("pe", lambda e: [
            e.matmul(ps_sq[:], ones_b[:, 0:1], sq[:, dc * SLAB:(dc + 1) * SLAB],
                     start=(dc == 0), stop=(dc == DCH - 1))
            for dc in range(DCH)][-1], deps=[e_sqr])
        e_mu = p.step("dve", lambda e: e.tensor_scalar_mul(mu_r[:], ps_su[:], 1.0 / D),
                      deps=[e_st1])
        e_vr = p.step("dve", lambda e: (
            e.tensor_mul(vr_r[:], mu_r[:], mu_r[:]),
            e.scalar_tensor_tensor(vr_r[:], ps_sq[:], 1.0 / D, vr_r[:],
                                   ALU.mult, ALU.subtract))[-1],
            deps=[e_mu, e_st2])
        e_rs = p.step("act", lambda e: (
            e.activation(vr_r[:], vr_r[:], AF.Ln, bias=cns[0:1, 0:1]),
            e.activation(vr_r[:], vr_r[:], AF.Exp, scale=cns[0:1, 1:2]))[-1], deps=[e_vr])
        e_bw = p.step("dve", lambda e: e.scalar_tensor_tensor(
            bw_r[:], mu_r[:], -1.0, vr_r[:], ALU.mult, ALU.mult), deps=[e_rs])
        e_ro = p.dma("pool", [(lnrow_d[0:1, :], vr_r[:]), (lnrow_d[1:2, :], bw_r[:])],
                     deps=[e_bw])
        e_bc = p.dma("pool", [(arep[:], lnrow_d[0:1, :].broadcast_to([128, SLAB])),
                              (brep[:], lnrow_d[1:2, :].broadcast_to([128, SLAB]))],
                     deps=[e_ro])
        e_xn = p.step("dve", lambda e: [
            (e.tensor_mul(z_t[:], xy[:, dc * SLAB:(dc + 1) * SLAB], arep[:]),
             e.tensor_add(xb[:, dc * SLAB:(dc + 1) * SLAB], z_t[:], brep[:]))
            for dc in range(DCH)][-1][-1], deps=[e_bc, e_st1, e_st2])
        e_y_all = []
        for m in range(DCH):
            ps = ps_a if m % 2 == 0 else ps_b
            e_mm = p.step("pe", lambda e, m=m, ps=ps: [
                e.matmul(ps[:], w_in_bf[:, k * D + m * 128:k * D + (m + 1) * 128],
                         xb[:, k * SLAB:(k + 1) * SLAB],
                         start=(k == 0), stop=(k == DCH - 1))
                for k in range(DCH)][-1],
                deps=[e_xn, ev_wcast[-1], last_xin_evac[m % 2]])
            e_ev = p.step("dve", lambda e, m=m, ps=ps: e.tensor_copy(
                xin_bf[:, m * SLAB:(m + 1) * SLAB], ps[:]), deps=[e_mm])
            e_y = p.step("dve", lambda e, m=m: e.scalar_tensor_tensor(
                xy[:, m * SLAB:(m + 1) * SLAB],
                xin_bf[:, m * SLAB:(m + 1) * SLAB], Dp_sc[:, m:m + 1],
                xy[:, m * SLAB:(m + 1) * SLAB], ALU.mult, ALU.add),
                deps=[e_ev])
            last_xin_evac[m % 2] = e_y
            e_y_all.append(e_y)
        e_y0d = p.dma("pool", [(y0_dram[dc * 128:(dc + 1) * 128, c0:c0 + SLAB],
                                xy[:, dc * SLAB:(dc + 1) * SLAB])
                               for dc in range(DCH)], deps=[e_y_all[-1]])
        e_y0_list.append(e_y0d)
        e_prev_y0 = e_y0d
        e_pdl = p.step("pe", lambda e: [
            e.matmul(ps_dl[:], wx_bf[:, k * 96:k * 96 + DTR],
                     xin_bf[:, k * SLAB:(k + 1) * SLAB],
                     start=(k == 0), stop=(k == DCH - 1))
            for k in range(DCH)][-1], deps=[e_y_all[-1], ev_wready])
        e_pB = p.step("pe", lambda e: [
            e.matmul(ps_B[:], wx_bf[:, k * 96 + DTR:k * 96 + DTR + NST],
                     xin_bf[:, k * SLAB:(k + 1) * SLAB],
                     start=(k == 0), stop=(k == DCH - 1))
            for k in range(DCH)][-1], deps=[e_pdl])
        e_pC = p.step("pe", lambda e: [
            e.matmul(ps_C[:], wx_bf[:, k * 96 + 80:k * 96 + 96],
                     xin_bf[:, k * SLAB:(k + 1) * SLAB],
                     start=(k == 0), stop=(k == DCH - 1))
            for k in range(DCH)][-1], deps=[e_pB])
        e_dlb = p.step("dve", lambda e: e.tensor_copy(dtl_bf[:], ps_dl[:]), deps=[e_pdl])
        e_Bsb = p.step("dve", lambda e: e.tensor_copy(B_sb[:], ps_B[:]), deps=[e_pB])
        e_Csb = p.step("dve", lambda e: e.tensor_copy(g_sb[:], ps_C[:]), deps=[e_pC])
        e_cdma = p.dma("pool", [(c_dram[:, c0:c0 + SLAB], g_sb[:])], deps=[e_Csb])
        e_dt_slab = []
        prev_sp2 = (None, 0)
        for m in range(DCH):
            ps = ps_a if m % 2 == 0 else ps_b
            e_mm = p.step("pe", lambda e, m=m, ps=ps: e.matmul(
                ps[:], wdt_bf[:, m * 128:(m + 1) * 128], dtl_bf[:],
                start=True, stop=True), deps=[e_dlb, last_xin_evac[m % 2]])
            e_sp1 = p.step("act", lambda e, m=m, ps=ps: (
                e.activation(r_t[:], ps[:], AF.Relu, bias=bdt_sc[:, m:m + 1]),
                e.activation(z_t[:], ps[:], AF.Abs, bias=bdt_sc[:, m:m + 1]),
                e.activation(z_t[:], z_t[:], AF.Exp, scale=cns[:, 2:3]),
                e.activation(z_t[:], z_t[:], AF.Ln, bias=cns[:, 3:4]))[-1],
                deps=[e_mm, e_rs, prev_sp2])
            e_sp2 = p.step("dve", lambda e, m=m, c0=c0: (
                e.tensor_add(r_t[:], r_t[:], z_t[:]),
                e.tensor_scalar(dt_all[:, m * T + c0: m * T + c0 + SLAB],
                                r_t[:], DT_MAX - DT_MIN, DT_MIN, ALU.mult, ALU.add),
                e.tensor_mul(sq[:, m * SLAB:(m + 1) * SLAB],
                             dt_all[:, m * T + c0: m * T + c0 + SLAB],
                             xin_bf[:, m * SLAB:(m + 1) * SLAB]))[-1],
                deps=[e_sp1])
            last_xin_evac[m % 2] = e_sp2
            prev_sp2 = e_sp2
            e_dt_slab.append(e_sp2)
        e_dt_done.append(e_dt_slab[-1])
        e_wg = p.step("pe", lambda e: [
            e.matmul(ps_g[:], A_bf[:, k * 16:(k + 1) * 16],
                     sq[:, k * SLAB:(k + 1) * SLAB],
                     start=(k == 0), stop=(k == DCH - 1))
            for k in range(DCH)][-1], deps=e_dt_slab + [e_A3])
        e_g = p.step("dve", lambda e: e.tensor_mul(g_sb[:], B_sb[:], ps_g[:]),
                     deps=[e_wg, e_cdma, e_Bsb])
        e_gd = p.dma("pool", [(g_dram[:, c0:c0 + SLAB], g_sb[:])], deps=[e_g])
        e_g_done.append(e_gd)
        fence = [e_wg, e_g, e_sp1, e_gd]

    e_p1 = e_g_done[-1]

    if stage < 4:
        p.dma("pool", [(dbg["dbg_dt"][:], dt_all[:])], deps=e_dt_done)
        p.dma("pool", [(dbg["dbg_g"][:], g_dram[:]), (dbg["dbg_c"][:], c_dram[:])],
              deps=e_g_done)
        p.dma("pool", [(dbg["dbg_u"][:], xin_bf[:])], deps=[e_p1])

    dt_ready = e_dt_done[-1]
    g_ready = e_g_done[-1]

    if stage >= 2:
        pfree("ps_g", "ps_C", "ps_B", "ps_dl", "ps_b", "ps_a", "ps_sq", "ps_su")
        free("wld", "r_t", "z_t", "g_sb", "B_sb", "dtl_bf", "brep", "arep",
             "bw_r", "vr_r", "mu_r", "xin_bf", "sq", "xb", "xy", "wdt_bf",
             "wx_bf", "w_in_bf")

        barAF = [alloc("barAF0", [128, T], F32), alloc("barAF1", [128, T], F32)]
        HF = [alloc("HF0", [128, T], F32), alloc("HF1", [128, T], F32)]
        grepF = alloc("grepF", [128, T], F32)
        ss_all = alloc("ss_all", [128, 64], F32)
        E_t = alloc("E_t", [128, 16], F32)
        rcv_sb = alloc("rcv_sb", [128, 8 * 136], F32)

        e_scan = [(None, 0), (None, 0)]
        e_cpA = {}
        e_cp_par = [(None, 0), (None, 0)]
        e_grepF = (None, 0)
        ui = 0
        for n in range(NST):
            e_grepF = p.dma("pool", [
                (grepF[:], g_dram[n:n + 1, :].broadcast_to([128, T]))],
                deps=[g_ready, e_scan[0], e_scan[1]])
            for dc in range(DCH):
                u = dc * 16 + n
                par = ui % 2
                e_bar = p.step("act", lambda e, dc=dc, u=u, par=par: e.activation(
                    barAF[par][:], dt_all[:, dc * T:(dc + 1) * T],
                    AF.Exp, scale=A_sc[:, u:u + 1]),
                    deps=[dt_ready, e_scan[par]])
                e_sc = p.step("dve", lambda e, par=par:
                              e.tensor_tensor_scan(HF[par][:], barAF[par][:],
                                                   grepF[:], 0.0,
                                                   ALU.mult, ALU.add),
                              deps=[e_bar, e_grepF, e_cp_par[par]])
                e_cpA[u] = p.step("pool", lambda e, par=par, u=u: e.tensor_copy(
                    Hb[:, u:u + 1], HF[par][:, T - 1:T]), deps=[e_sc])
                e_cp_par[par] = e_cpA[u]
                e_scan[par] = e_sc
                ui += 1

        e_sdt = p.step("dve", lambda e: [
            e.tensor_reduce(Sdt_sb[:, dc:dc + 1], dt_all[:, dc * T:(dc + 1) * T],
                            mybir.AxisListType.X, ALU.add)
            for dc in range(DCH)][-1], deps=[dt_ready, e_scan[0], e_scan[1]])
        e_snd = p.dma("pool", [(snd_d.ap()[:, 0:128], Hb[:]),
                               (snd_d.ap()[:, 128:136], Sdt_sb[:])],
                      deps=[e_sdt, e_scan[0], e_scan[1]])
        e_cc = p.step("pool", lambda e: e.collective_compute(
            "AllGather", ALU.bypass, replica_groups=[list(range(NCORES))],
            ins=[snd_d.ap().opt()], outs=[rcv_d.ap().opt()]),
            deps=[e_snd], sem="cc")
        e_rcv = p.dma("pool", [(rcv_sb[:, j * 136:(j + 1) * 136],
                                rcv_d.ap()[j * 128:(j + 1) * 128, :])
                               for j in range(NCORES)], deps=[e_cc])
        e_hin0 = p.step("dve", lambda e: e.memset(hin[:], 0.0), deps=[])
        e_ss = []
        for j in range(NCORES):
            e_c = p.step("dve", lambda e, j=j: e.tensor_scalar(
                ss_all[:, j * 8:(j + 1) * 8], ones8[:],
                m2b_sb[:, j * 9 + 8:j * 9 + 9], None, ALU.mult),
                deps=[e_rcv, e_ones8, e_cl])
            for i in range(NCORES):
                e_c = p.step("dve", lambda e, j=j, i=i: e.scalar_tensor_tensor(
                    ss_all[:, j * 8:(j + 1) * 8],
                    rcv_sb[:, i * 136 + 128:i * 136 + 136],
                    m2b_sb[:, j * 9 + i:j * 9 + i + 1],
                    ss_all[:, j * 8:(j + 1) * 8], ALU.mult, ALU.add),
                    deps=[e_c])
            e_ss.append(e_c)
        e_comb = e_hin0
        for j in range(NCORES):
            for dc in range(DCH):
                e_E = p.step("act", lambda e, j=j, dc=dc: e.activation(
                    E_t[:], A_sc[:, dc * 16:(dc + 1) * 16], AF.Exp,
                    scale=ss_all[:, j * 8 + dc:j * 8 + dc + 1]),
                    deps=[e_ss[j], e_comb])
                e_mu2 = p.step("dve", lambda e, j=j, dc=dc: e.tensor_mul(
                    E_t[:], E_t[:],
                    rcv_sb[:, j * 136 + dc * 16:j * 136 + (dc + 1) * 16]),
                    deps=[e_E, e_hin0])
                e_comb = p.step("dve", lambda e, dc=dc: e.tensor_add(
                    hin[:, dc * 16:(dc + 1) * 16],
                    hin[:, dc * 16:(dc + 1) * 16], E_t[:]),
                    deps=[e_mu2])
        e_hin = e_comb

        if stage < 4:
            p.dma("pool", [(dbg["dbg_hb"][:], Hb[:])], deps=[e_snd])
            p.dma("pool", [(dbg["dbg_hin"][:], hin[:])], deps=[e_hin])

    if stage >= 3:
        free("rcv_sb", "E_t", "ss_all", "grepF", "HF1", "HF0", "barAF1", "barAF0")

        Yq = alloc("Yq", [128, DCH * SLAB], F32)
        h1q = alloc("h1q", [128, MH * SLAB], BF16)
        h2q = alloc("h2q", [128, MH * SLAB], BF16)
        barQ = [alloc("barQ0", [128, SLAB], F32), alloc("barQ1", [128, SLAB], F32)]
        HQ = [alloc("HQ0", [128, SLAB], F32), alloc("HQ1", [128, SLAB], F32)]
        gq = alloc("gq", [128, SLAB], F32)
        cq = alloc("cq", [128, SLAB], F32)
        tmpq = alloc("tmpq", [128, SLAB], F32)
        xb2 = alloc("xb2", [128, DCH * SLAB], BF16)
        mu2_r = alloc("mu2_r", [1, SLAB], F32)
        vr2_r = alloc("vr2_r", [1, SLAB], F32)
        bw2_r = alloc("bw2_r", [1, SLAB], F32)
        ar2 = alloc("ar2", [128, SLAB], F32)
        br2 = alloc("br2", [128, SLAB], F32)
        zz2 = alloc("zz2", [128, SLAB], F32)
        stq = [alloc("stq0", [128, SLAB], BF16), alloc("stq1", [128, SLAB], BF16)]
        t2q = [alloc("t2q0", [128, SLAB], BF16), alloc("t2q1", [128, SLAB], BF16)]
        obq = [alloc("obq0", [128, SLAB], F32), alloc("obq1", [128, SLAB], F32)]
        xmq = [alloc("xmq0", [128, SLAB], F32), alloc("xmq1", [128, SLAB], F32)]
        ps2su = palloc("ps2su", [1, SLAB])
        ps2sq = palloc("ps2sq", [1, SLAB])
        ps_fa = [palloc("ps_fa0", [128, SLAB]), palloc("ps_fa1", [128, SLAB])]
        ps_fb = [palloc("ps_fb0", [128, SLAB]), palloc("ps_fb1", [128, SLAB])]

        e_scanB = [(None, 0), (None, 0)]
        e_xm_list = []
        e_cpB = {}
        e_cpB_par = [(None, 0), (None, 0)]
        e_ymul = [(None, 0), (None, 0)]
        e_yq_prev = (None, 0)
        e_xm_prev = (None, 0)
        e_xn2_prev = {}
        e_l1mm_prev = {}
        e_l4_prev = (None, 0)
        e_prevbc2 = (None, 0)
        e_vr2_prev = (None, 0)
        e_wpar = {}
        e_evA = {}
        e_evB = {}
        e_out_total = None
        do_ffn = stage >= 4

        for q in range(NSL):
            c0 = q * SLAB
            e_y0in = p.dma("pool", [(Yq[:, dc * SLAB:(dc + 1) * SLAB],
                                     y0_dram[dc * 128:(dc + 1) * 128, c0:c0 + SLAB])
                                    for dc in range(DCH)],
                           deps=[e_yq_prev, e_y0_list[q], e_xm_prev])
            for n in range(NST):
                dep_rep = [e_scanB[0], e_scanB[1], e_ymul[0], e_ymul[1], e_hin]
                e_gq = p.dma("pool", [(gq[:], g_dram[n:n + 1, c0:c0 + SLAB]
                                       .broadcast_to([128, SLAB]))], deps=dep_rep)
                e_cq = p.dma("pool", [(cq[:], c_dram[n:n + 1, c0:c0 + SLAB]
                                       .broadcast_to([128, SLAB]))], deps=dep_rep)
                for dc in range(DCH):
                    u = dc * 16 + n
                    par = (q * 128 + n * DCH + dc) % 2
                    e_bar = p.step("act", lambda e, dc=dc, u=u, par=par, c0=c0:
                                   e.activation(barQ[par][:],
                                                dt_all[:, dc * T + c0: dc * T + c0 + SLAB],
                                                AF.Exp, scale=A_sc[:, u:u + 1]),
                                   deps=[e_scanB[par], e_hin])
                    init = hin[:, u:u + 1] if q == 0 else Hb[:, u:u + 1]
                    dep_sc = [e_bar, e_gq, e_cq, e_cpB_par[par], e_ymul[par]]
                    if q > 0:
                        dep_sc.append(e_cpB[u])
                    e_sc = p.step("dve", lambda e, par=par, init=init:
                                  e.tensor_tensor_scan(HQ[par][:], barQ[par][:],
                                                       gq[:], init,
                                                       ALU.mult, ALU.add),
                                  deps=dep_sc)
                    if q < NSL - 1:
                        e_cpB[u] = p.step("pool", lambda e, par=par, u=u: e.tensor_copy(
                            Hb[:, u:u + 1], HQ[par][:, SLAB - 1:SLAB]), deps=[e_sc])
                        e_cpB_par[par] = e_cpB[u]
                    e_ym = p.step("dve", lambda e, par=par, dc=dc: (
                        e.tensor_mul(tmpq[:], HQ[par][:], cq[:]),
                        e.tensor_add(Yq[:, dc * SLAB:(dc + 1) * SLAB],
                                     Yq[:, dc * SLAB:(dc + 1) * SLAB], tmpq[:]))[-1],
                        deps=[e_sc, e_y0in])
                    e_ymul[par] = e_ym
                    e_scanB[par] = e_sc
            e_passBq = [e_ymul[0], e_ymul[1]]

            Xfq = Xf2[q % 2]
            e_cast2 = p.step("dve", lambda e: e.tensor_copy(xb2[:], Yq[:]),
                             deps=e_passBq + [e_prevbc2])
            e_st1 = p.step("pe", lambda e: [
                e.matmul(ps2su[:], ones_b[:, 0:1], xb2[:, dc * SLAB:(dc + 1) * SLAB],
                         start=(dc == 0), stop=(dc == DCH - 1))
                for dc in range(DCH)][-1], deps=[e_cast2, e_vr2_prev])
            e_sqr2 = p.step("act", lambda e: e.activation(xb2[:], Yq[:], AF.Square),
                            deps=[e_st1] + e_passBq)
            e_st2 = p.step("pe", lambda e: [
                e.matmul(ps2sq[:], ones_b[:, 0:1], xb2[:, dc * SLAB:(dc + 1) * SLAB],
                         start=(dc == 0), stop=(dc == DCH - 1))
                for dc in range(DCH)][-1], deps=[e_sqr2])
            e_mu2 = p.step("dve", lambda e: e.tensor_scalar_mul(mu2_r[:], ps2su[:], 1.0 / D),
                           deps=[e_st1])
            e_vr2 = p.step("dve", lambda e: (
                e.tensor_mul(vr2_r[:], mu2_r[:], mu2_r[:]),
                e.scalar_tensor_tensor(vr2_r[:], ps2sq[:], 1.0 / D, vr2_r[:],
                                       ALU.mult, ALU.subtract))[-1],
                deps=[e_mu2, e_st2])
            e_vr2_prev = e_vr2
            e_rs2 = p.step("act", lambda e: (
                e.activation(vr2_r[:], vr2_r[:], AF.Ln, bias=cns[0:1, 0:1]),
                e.activation(vr2_r[:], vr2_r[:], AF.Exp, scale=cns[0:1, 1:2]))[-1],
                deps=[e_vr2])
            e_bw2 = p.step("dve", lambda e: e.scalar_tensor_tensor(
                bw2_r[:], mu2_r[:], -1.0, vr2_r[:], ALU.mult, ALU.mult), deps=[e_rs2])
            e_ro2 = p.dma("pool", [(lnrow_d[0:1, :], vr2_r[:]),
                                   (lnrow_d[1:2, :], bw2_r[:])], deps=[e_bw2])
            e_bc2 = p.dma("pool", [(ar2[:], lnrow_d[0:1, :].broadcast_to([128, SLAB])),
                                   (br2[:], lnrow_d[1:2, :].broadcast_to([128, SLAB]))],
                          deps=[e_ro2])
            e_prevbc2 = e_bc2
            e_xn2 = p.step("dve", lambda e, Xfq=Xfq: [
                (e.tensor_mul(zz2[:], Yq[:, dc * SLAB:(dc + 1) * SLAB], ar2[:]),
                 e.tensor_add(Xfq[:, dc * SLAB:(dc + 1) * SLAB], zz2[:], br2[:]))
                for dc in range(DCH)][-1][-1],
                deps=[e_bc2, e_st1, e_st2, e_l1mm_prev.get(q % 2, (None, 0))])
            e_xm = p.dma("pool", [(xmid_d[dc * 128:(dc + 1) * 128, c0:c0 + SLAB],
                                   Yq[:, dc * SLAB:(dc + 1) * SLAB])
                                  for dc in range(DCH)], deps=[e_xn2])
            e_xn2_prev[q % 2] = e_xn2
            e_yq_prev = e_xn2
            e_xm_prev = e_xm
            e_xm_list.append(e_xm)

            if not do_ffn:
                continue

            for m in range(MH):
                wp = m % 2
                e_w = p.dma("sp", [
                    (wcur[wp][:, 0:1024], w1m_h[m].rearrange("p k c -> p (k c)")),
                    (wcur[wp][:, 1024:2048], w2m_h[m].rearrange("p k c -> p (k c)"))],
                    deps=[e_wpar.get(wp, (None, 0))])
                pp = m % 2
                e_mA = p.step("pe", lambda e, wp=wp, pp=pp, Xfq=Xfq: [
                    e.matmul(ps_fa[pp][:], wcur[wp][:, k * 128:(k + 1) * 128],
                             Xfq[:, k * SLAB:(k + 1) * SLAB],
                             start=(k == 0), stop=(k == DCH - 1))
                    for k in range(DCH)][-1],
                    deps=[e_w, e_xn2, e_evA.get(pp, (None, 0))])
                e_mB = p.step("pe", lambda e, wp=wp, pp=pp, Xfq=Xfq: [
                    e.matmul(ps_fb[pp][:], wcur[wp][:, 1024 + k * 128:1024 + (k + 1) * 128],
                             Xfq[:, k * SLAB:(k + 1) * SLAB],
                             start=(k == 0), stop=(k == DCH - 1))
                    for k in range(DCH)][-1],
                    deps=[e_w, e_evB.get(pp, (None, 0))])
                e_s = p.step("act", lambda e, m=m, pp=pp: (
                    e.activation(stq[pp][:], ps_fa[pp][:], AF.Silu,
                                 scale=s1_sc[:, m:m + 1]),
                    e.activation(t2q[pp][:], ps_fb[pp][:], AF.Copy,
                                 scale=s2_sc[:, m:m + 1]))[-1],
                    deps=[e_mA, e_mB])
                e_h = p.step("dve", lambda e, m=m, pp=pp: e.tensor_mul(
                    h1q[:, m * SLAB:(m + 1) * SLAB], stq[pp][:], t2q[pp][:]),
                    deps=[e_s, e_l4_prev])
                e_evA[pp] = e_h
                e_evB[pp] = e_h
                e_wpar[wp] = e_mB
            e_l1mm_prev[q % 2] = e_mB
            l1_done = [e_evA[0], e_evA[1]]
            for layer, (wm_h, sc_sb, src, dst) in enumerate([
                    (h1m_h, sh1_sc, h1q, h2q), (h2m_h, sh2_sc, h2q, h1q)]):
                prev_dep = l1_done if layer == 0 else [e_evA[0], e_evA[1]]
                for m in range(MH):
                    wp = m % 2
                    e_w = p.dma("sp", [(wcur[wp][:], wm_h[m].rearrange("p k c -> p (k c)"))],
                                deps=[e_wpar.get(wp, (None, 0))])
                    pp = m % 2
                    e_mm = p.step("pe", lambda e, wp=wp, pp=pp, src=src: [
                        e.matmul(ps_fa[pp][:], wcur[wp][:, k * 128:(k + 1) * 128],
                                 src[:, k * SLAB:(k + 1) * SLAB],
                                 start=(k == 0), stop=(k == MH - 1))
                        for k in range(MH)][-1],
                        deps=[e_w, e_evA.get(pp, (None, 0))] + prev_dep)
                    e_g2 = p.step("act", lambda e, m=m, pp=pp, dst=dst, sc_sb=sc_sb:
                                  e.activation(dst[:, m * SLAB:(m + 1) * SLAB],
                                               ps_fa[pp][:], AF.Gelu,
                                               scale=sc_sb[:, m:m + 1]),
                                  deps=[e_mm])
                    e_evA[pp] = e_g2
                    e_wpar[wp] = e_mm
            for m in range(DCH):
                wp = m % 2
                e_w = p.dma("sp", [(wcur[wp][:], w3m_h[m].rearrange("p k c -> p (k c)"))],
                            deps=[e_wpar.get(wp, (None, 0))])
                pp = m % 2
                e_mm = p.step("pe", lambda e, wp=wp, pp=pp: [
                    e.matmul(ps_fa[pp][:], wcur[wp][:, k * 128:(k + 1) * 128],
                             h1q[:, k * SLAB:(k + 1) * SLAB],
                             start=(k == 0), stop=(k == MH - 1))
                    for k in range(MH)][-1],
                    deps=[e_w, e_evA.get(pp, (None, 0)),
                          e_evB.get((9, pp), (None, 0))])
                e_o = p.step("act", lambda e, m=m, pp=pp: e.activation(
                    obq[pp][:], ps_fa[pp][:], AF.Copy, scale=s3_sc[:, m:m + 1]),
                    deps=[e_mm, e_evB.get((9, pp), (None, 0))])
                e_evA[pp] = e_o
                e_xmi = p.dma("pool", [(xmq[pp][:],
                                        xmid_d[m * 128:(m + 1) * 128, c0:c0 + SLAB])],
                              deps=[e_xm, e_o, e_evB.get((8, pp), (None, 0))])
                e_add = p.step("dve", lambda e, pp=pp: e.tensor_add(
                    obq[pp][:], obq[pp][:], xmq[pp][:]), deps=[e_o, e_xmi])
                e_evB[(8, pp)] = e_add
                e_od = p.dma("pool", [(out_h[m * 128:(m + 1) * 128, c0:c0 + SLAB],
                                       obq[pp][:])], deps=[e_add])
                e_evB[(9, pp)] = e_od
                e_out_total = e_od
                e_wpar[wp] = e_mm
            e_l4_prev = e_mm

        if stage in (3, 5):
            p.dma("pool", [(dbg["dbg_xmid"][:], xmid_d[:])],
                  deps=e_xm_list)

    if stage >= 4:
        assert e_out_total is not None
        p.wait_only("pool", [(s, p.cnt[s]) for s in p.dma_pools["pool"]])
    else:
        p.wait_only("pool", [(s, p.cnt[s]) for s in p.dma_pools["pool"]])

    with nc.Block() as block:
        p.emit(block)

    return nc


def _shard_inputs(inputs):
    x = np.ascontiguousarray(inputs["L0_attn_x"])
    bf = ml_dtypes.bfloat16

    def tile_w(w, kk, mm):
        o, i = w.shape
        r = w.reshape(mm, 128, kk, 128)
        return np.ascontiguousarray(r.transpose(0, 3, 2, 1)).astype(bf)

    w1m = tile_w(np.asarray(inputs["L0_ffn_w1_w"], np.float32), 8, 32)
    w2m = tile_w(np.asarray(inputs["L0_ffn_w2_w"], np.float32), 8, 32)
    h1m = tile_w(np.asarray(inputs["L0_ffn_h1_w"], np.float32), 32, 32)
    h2m = tile_w(np.asarray(inputs["L0_ffn_h2_w"], np.float32), 32, 32)
    w3m = tile_w(np.asarray(inputs["L0_ffn_w3_w"], np.float32), 32, 8)

    def cols(v, k):
        return np.ascontiguousarray(np.asarray(v, np.float32).reshape(k, 128).T)

    common = dict(
        w_in=np.ascontiguousarray(np.asarray(inputs["L0_attn_w_in"], np.float32)),
        w_x=np.ascontiguousarray(np.asarray(inputs["L0_attn_w_x"], np.float32)),
        w_dt=np.ascontiguousarray(np.asarray(inputs["L0_attn_w_dt"], np.float32)),
        b_dt=cols(inputs["L0_attn_b_dt"], 8),
        A_log=np.ascontiguousarray(np.asarray(inputs["L0_attn_A_log"], np.float32)),
        Dp=cols(inputs["L0_attn_Dp"], 8),
        w1m=w1m, w2m=w2m, h1m=h1m, h2m=h2m, w3m=w3m,
        s1=cols(inputs["L0_ffn_w1_s"], 32),
        s2=cols(inputs["L0_ffn_w2_s"], 32),
        sh1=cols(inputs["L0_ffn_h1_s"], 32),
        sh2=cols(inputs["L0_ffn_h2_s"], 32),
        s3=cols(inputs["L0_ffn_w3_s"], 8),
        onesc=np.ones((128, 1), np.float32),
        cns=np.ascontiguousarray(np.broadcast_to(
            np.array([1e-5, -0.5, -1.0, 1.0], np.float32), (128, 4))),
    )

    in_maps = []
    for c in range(NCORES):
        bat, pos = c // 4, c % 4
        xs = np.ascontiguousarray(
            np.asarray(x[bat, pos * T:(pos + 1) * T, :], np.float32).T)
        m2b = np.zeros((8, 9), np.float32)
        for j in range(8):
            incl = (j // 4 == bat) and (j % 4 < pos)
            m2b[j, 8] = 0.0 if incl else BIG
            for i in range(8):
                m2b[j, i] = 1.0 if (incl and i // 4 == bat
                                    and j % 4 < i % 4 < pos) else 0.0
        m2b_t = np.ascontiguousarray(
            np.broadcast_to(m2b.reshape(1, 72), (128, 72)))
        im = dict(common)
        im["x"] = xs
        im["m2b"] = m2b_t
        in_maps.append(im)
    return in_maps


def kernel(**inputs):
    in_maps = _shard_inputs(inputs)
    nc = build_program(4)
    res = run_bass_kernel_spmd(nc, in_maps, core_ids=list(range(NCORES)))
    out = np.empty((2, 8192, 1024), np.float32)
    for c in range(NCORES):
        bat, pos = c // 4, c % 4
        out[bat, pos * T:(pos + 1) * T, :] = res.results[c]["out"].T
    return out


# version 05
# speedup vs baseline: 1.1389x; 1.1389x over previous
import numpy as np
import ml_dtypes
import concourse.bass as bass
import concourse.mybir as mybir
from concourse.bass_utils import run_bass_kernel_spmd

F32 = mybir.dt.float32
BF16 = mybir.dt.bfloat16
AF = mybir.ActivationFunctionType
ALU = mybir.AluOpType

NCORES = 8
D = 1024
T = 2048
NST = 16
DTR = 64
DCH = 8
HID = 4096
MH = HID // 128
SLAB = 512
NSL = T // SLAB
BIG = 1.0e30
DT_MIN, DT_MAX = 0.001, 0.1


class Prog:
    def __init__(self, nc):
        self.nc = nc
        self.events = []
        self.cnt = {}
        self.sems = {}
        self.guards = []

    def mksem(self, name):
        g = self.nc.semaphore(name)
        h = g.__enter__()
        self.guards.append(g)
        self.sems[name] = h
        self.cnt[name] = 0

    def step(self, eng, emit, deps=(), inc=1, sem=None):
        sname = sem or eng
        waits = [(self.sems[s], v) for (s, v) in deps if v > 0]
        if inc:
            self.cnt[sname] += inc
        after = self.cnt[sname]
        smh = self.sems[sname]

        def fn(e, waits=waits, inc=inc, smh=smh, emit=emit):
            for h, v in waits:
                e.wait_ge(h, v)
            inst = emit(e)
            if inc:
                inst.then_inc(smh, inc)

        self.events.append((eng, fn))
        return (sname, after)

    def dma(self, eng, transfers, deps=(), sem=None):
        if sem is None:
            pool = self.dma_pools[eng]
            sname = pool[self.dma_rr[eng] % len(pool)]
            self.dma_rr[eng] += 1
        else:
            sname = sem
        waits = [(self.sems[s], v) for (s, v) in deps if v > 0]
        prev_total = self.cnt[sname]
        self.cnt[sname] += 16 * len(transfers)
        after = self.cnt[sname]
        smh = self.sems[sname]

        def fn(e, waits=waits, transfers=transfers, smh=smh, prev=prev_total):
            if prev > 0:
                e.wait_ge(smh, prev)
            for h, v in waits:
                e.wait_ge(h, v)
            for out, in_ in transfers:
                e.dma_start(out=out, in_=in_).then_inc(smh, 16)

        self.events.append((eng, fn))
        return (sname, after)

    def wait_only(self, eng, deps):
        waits = [(self.sems[s], v) for (s, v) in deps if v > 0]

        def fn(e, waits=waits):
            for h, v in waits:
                e.wait_ge(h, v)

        self.events.append((eng, fn))

    def emit(self, block):
        evs = self.events

        @block.sync
        def _(e):
            for k, fn in evs:
                if k == "sp":
                    fn(e)

        @block.tensor
        def _(e):
            for k, fn in evs:
                if k == "pe":
                    fn(e)

        @block.scalar
        def _(e):
            for k, fn in evs:
                if k == "act":
                    fn(e)

        @block.vector
        def _(e):
            for k, fn in evs:
                if k == "dve":
                    fn(e)

        @block.gpsimd
        def _(e):
            for k, fn in evs:
                if k == "pool":
                    fn(e)


def build_program(stage=4):
    full = stage >= 4
    nc = bass.Bass("TRN2", target_bir_lowering=False, detect_race_conditions=False)

    def din(name, shape, dt=F32):
        return nc.dram_tensor(name, list(shape), dt, kind="ExternalInput").ap()

    x_h = din("x", [D, T])
    w_in_h = din("w_in", [D, D])
    w_x_h = din("w_x", [D, 96])
    w_dt_h = din("w_dt", [DTR, D])
    bdt_h = din("b_dt", [128, DCH])
    Alog_h = din("A_log", [D, NST])
    Dp_h = din("Dp", [128, DCH])
    w1m_h = din("w1m", [32, 128, 8, 128], BF16)
    w2m_h = din("w2m", [32, 128, 8, 128], BF16)
    h1m_h = din("h1m", [32, 128, 32, 128], BF16)
    h2m_h = din("h2m", [32, 128, 32, 128], BF16)
    w3m_h = din("w3m", [8, 128, 32, 128], BF16)
    s1_h = din("s1", [128, 32])
    s2_h = din("s2", [128, 32])
    sh1_h = din("sh1", [128, 32])
    sh2_h = din("sh2", [128, 32])
    s3_h = din("s3", [128, DCH])
    m2b_h = din("m2b", [128, 72])
    ones_h = din("onesc", [128, 1])
    cns_h = din("cns", [128, 4])

    out_h = nc.dram_tensor("out", [D, T], F32, kind="ExternalOutput").ap()
    dbg = {}
    if stage == 5:
        dbg["dbg_xmid"] = nc.dram_tensor("dbg_xmid", [D, T], F32, kind="ExternalOutput").ap()
    if stage < 4:
        def dout(name, shape, dt=F32):
            dbg[name] = nc.dram_tensor(name, list(shape), dt, kind="ExternalOutput").ap()
            return dbg[name]
        dout("dbg_dt", [128, DCH * T], BF16)
        dout("dbg_u", [128, DCH * SLAB], BF16)
        dout("dbg_g", [NST, T])
        dout("dbg_c", [NST, T])
        if stage >= 2:
            dout("dbg_hb", [128, 128])
            dout("dbg_hin", [128, 128])
        if stage >= 3:
            dout("dbg_xmid", [D, T])

    g_dram = nc.dram_tensor("g_scr", [NST, T], F32).ap()
    c_dram = nc.dram_tensor("c_scr", [NST, T], F32).ap()
    y0_dram = nc.dram_tensor("y0_scr", [D, T], F32).ap()
    snd_d = nc.dram_tensor("snd_scr", [128, 136], F32)
    rcv_d = nc.dram_tensor("rcv_scr", [128 * NCORES, 136], F32)
    xmid_d = nc.dram_tensor("xmid_scr", [D, T], F32).ap()
    lnrow_d = nc.dram_tensor("lnrow_scr", [2, SLAB], F32).ap()

    p = Prog(nc)
    for s in ["pe", "act", "dve", "pool", "sp", "cc"]:
        p.mksem(s)
    p.dma_pools = {"sp": [], "pool": []}
    p.dma_rr = {"sp": 0, "pool": 0}
    for i in range(6):
        nm = "spq%d" % i
        p.mksem(nm)
        p.dma_pools["sp"].append(nm)
    for i in range(8):
        nm = "plq%d" % i
        p.mksem(nm)
        p.dma_pools["pool"].append(nm)

    live = {}

    def alloc(name, shape, dt, side="left"):
        g = nc.sbuf_tensor(name, list(shape), dt, side=side)
        t = g.__enter__()
        live[name] = g
        return t

    def free(*names):
        for n in names:
            live.pop(n).__exit__(None, None, None)

    palloc_guards = {}

    def palloc(name, shape, dt=F32):
        g = nc.psum_tensor(name, list(shape), dt)
        t = g.__enter__()
        palloc_guards[name] = g
        return t

    def pfree(*names):
        for n in names:
            palloc_guards.pop(n).__exit__(None, None, None)

    A_sc = alloc("A_sc", [128, 128], F32)
    A_bf = alloc("A_bf", [128, 128], BF16)
    bdt_sc = alloc("bdt_sc", [128, DCH], F32)
    Dp_sc = alloc("Dp_sc", [128, DCH], F32)
    s1_sc = alloc("s1_sc", [128, 32], F32)
    s2_sc = alloc("s2_sc", [128, 32], F32)
    sh1_sc = alloc("sh1_sc", [128, 32], F32)
    sh2_sc = alloc("sh2_sc", [128, 32], F32)
    s3_sc = alloc("s3_sc", [128, DCH], F32)
    m2b_sb = alloc("m2b_sb", [128, 72], F32)
    ones_f = alloc("ones_f", [128, 1], F32)
    ones_b = alloc("ones_b", [128, 1], BF16)
    cns = alloc("cns_sb", [128, 4], F32)
    ones8 = alloc("ones8", [128, 8], F32)
    Hb = alloc("Hb", [128, 128], F32)
    hin = alloc("hin", [128, 128], F32)
    Sdt_sb = alloc("Sdt_sb", [128, DCH], F32)

    wcur = [alloc("wcur0", [128, 4096], BF16, side="right"),
            alloc("wcur1", [128, 4096], BF16, side="right")]
    Xf2 = [alloc("Xf0", [128, DCH * SLAB], BF16, side="right"),
           alloc("Xf1", [128, DCH * SLAB], BF16, side="right")]

    dt_all = alloc("dt_all", [128, DCH * T], BF16)

    e_cl = p.dma("sp", [
        (bdt_sc[:], bdt_h[:]), (Dp_sc[:], Dp_h[:]),
        (s1_sc[:], s1_h[:]), (s2_sc[:], s2_h[:]),
        (sh1_sc[:], sh1_h[:]), (sh2_sc[:], sh2_h[:]), (s3_sc[:], s3_h[:]),
        (m2b_sb[:], m2b_h[:]), (ones_f[:], ones_h[:]), (cns[:], cns_h[:]),
    ] + [(A_sc[:, dc * 16:(dc + 1) * 16], Alog_h[dc * 128:(dc + 1) * 128, :])
         for dc in range(DCH)])
    e_A1 = p.step("act", lambda e: e.activation(A_sc[:], A_sc[:], AF.Exp), deps=[e_cl])
    e_A2 = p.step("dve", lambda e: e.tensor_scalar_mul(A_sc[:], A_sc[:], -1.0), deps=[e_A1])
    e_A3 = p.step("dve", lambda e: e.tensor_copy(A_bf[:], A_sc[:]), deps=[e_A2])
    e_ones = p.step("dve", lambda e: e.tensor_copy(ones_b[:], ones_f[:]), deps=[e_cl])
    e_ones8 = p.step("dve", lambda e: e.memset(ones8[:], 1.0), deps=[])

    w_in_bf = alloc("w_in_bf", [128, DCH * D], BF16)
    wx_bf = alloc("wx_bf", [128, DCH * 96], BF16)
    wdt_bf = alloc("wdt_bf", [DTR, D], BF16)
    xy = alloc("xy", [128, DCH * SLAB], F32)
    xb = alloc("xb", [128, DCH * SLAB], BF16)
    sq = alloc("sq", [128, DCH * SLAB], BF16)
    xin_bf = alloc("xin_bf", [128, DCH * SLAB], BF16)
    mu_r = alloc("mu_r", [1, SLAB], F32)
    vr_r = alloc("vr_r", [1, SLAB], F32)
    bw_r = alloc("bw_r", [1, SLAB], F32)
    arep = alloc("arep", [128, SLAB], F32)
    brep = alloc("brep", [128, SLAB], F32)
    dtl_bf = alloc("dtl_bf", [DTR, SLAB], BF16)
    B_sb = alloc("B_sb", [NST, SLAB], F32)
    g_sb = alloc("g_sb", [NST, SLAB], F32)
    z_t = alloc("z_t", [128, SLAB], F32)
    r_t = alloc("r_t", [128, SLAB], F32)
    wld = alloc("wld", [128, D], F32)

    ev_wcast = []
    prev_cast = (None, 0)
    for k in range(DCH):
        dep = [] if k == 0 else [prev_cast]
        e_l = p.dma("sp", [(wld[:], w_in_h[k * 128:(k + 1) * 128, :])], deps=dep)
        prev_cast = p.step("dve", lambda e, k=k: e.tensor_copy(
            w_in_bf[:, k * D:(k + 1) * D], wld[:]), deps=[e_l])
        ev_wcast.append(prev_cast)
    for k in range(DCH):
        e_l = p.dma("sp", [(wld[:, 0:96], w_x_h[k * 128:(k + 1) * 128, :])], deps=[prev_cast])
        prev_cast = p.step("dve", lambda e, k=k: e.tensor_copy(
            wx_bf[:, k * 96:(k + 1) * 96], wld[:, 0:96]), deps=[e_l])
    e_l = p.dma("sp", [(wld[0:DTR, :], w_dt_h[:])], deps=[prev_cast])
    e_wdt = p.step("dve", lambda e: e.tensor_copy(wdt_bf[:], wld[0:DTR, :]), deps=[e_l])
    ev_wready = e_wdt

    ps_su = palloc("ps_su", [1, SLAB])
    ps_sq = palloc("ps_sq", [1, SLAB])
    ps_a = palloc("ps_a", [128, SLAB])
    ps_b = palloc("ps_b", [128, SLAB])
    ps_dl = palloc("ps_dl", [DTR, SLAB])
    ps_B = palloc("ps_B", [NST, SLAB])
    ps_C = palloc("ps_C", [NST, SLAB])
    ps_g = palloc("ps_g", [NST, SLAB])

    e_dt_done = []
    e_g_done = []
    e_y0_list = []
    e_prev_y0 = (None, 0)
    last_xin_evac = [(None, 0), (None, 0)]
    fence = []

    for s in range(NSL):
        c0 = s * SLAB

        e_x = p.dma("pool", [(xy[:, dc * SLAB:(dc + 1) * SLAB],
                              x_h[dc * 128:(dc + 1) * 128, c0:c0 + SLAB])
                             for dc in range(DCH)], deps=[e_prev_y0])
        e_cast = p.step("dve", lambda e: e.tensor_copy(xb[:], xy[:]),
                        deps=[e_x, e_ones] + fence)
        e_sqr = p.step("act", lambda e: e.activation(sq[:], xy[:], AF.Square),
                       deps=[e_x, e_A1] + fence)
        e_st1 = p.step("pe", lambda e: [
            e.matmul(ps_su[:], ones_b[:, 0:1], xb[:, dc * SLAB:(dc + 1) * SLAB],
                     start=(dc == 0), stop=(dc == DCH - 1))
            for dc in range(DCH)][-1], deps=[e_cast])
        e_st2 = p.step("pe", lambda e: [
            e.matmul(ps_sq[:], ones_b[:, 0:1], sq[:, dc * SLAB:(dc + 1) * SLAB],
                     start=(dc == 0), stop=(dc == DCH - 1))
            for dc in range(DCH)][-1], deps=[e_sqr])
        e_mu = p.step("dve", lambda e: e.tensor_scalar_mul(mu_r[:], ps_su[:], 1.0 / D),
                      deps=[e_st1])
        e_vr = p.step("dve", lambda e: (
            e.tensor_mul(vr_r[:], mu_r[:], mu_r[:]),
            e.scalar_tensor_tensor(vr_r[:], ps_sq[:], 1.0 / D, vr_r[:],
                                   ALU.mult, ALU.subtract))[-1],
            deps=[e_mu, e_st2])
        e_rs = p.step("act", lambda e: (
            e.activation(vr_r[:], vr_r[:], AF.Ln, bias=cns[0:1, 0:1]),
            e.activation(vr_r[:], vr_r[:], AF.Exp, scale=cns[0:1, 1:2]))[-1], deps=[e_vr])
        e_bw = p.step("dve", lambda e: e.scalar_tensor_tensor(
            bw_r[:], mu_r[:], -1.0, vr_r[:], ALU.mult, ALU.mult), deps=[e_rs])
        e_ro = p.dma("pool", [(lnrow_d[0:1, :], vr_r[:]), (lnrow_d[1:2, :], bw_r[:])],
                     deps=[e_bw])
        e_bc = p.dma("pool", [(arep[:], lnrow_d[0:1, :].broadcast_to([128, SLAB])),
                              (brep[:], lnrow_d[1:2, :].broadcast_to([128, SLAB]))],
                     deps=[e_ro])
        e_xn = p.step("dve", lambda e: [
            (e.tensor_mul(z_t[:], xy[:, dc * SLAB:(dc + 1) * SLAB], arep[:]),
             e.tensor_add(xb[:, dc * SLAB:(dc + 1) * SLAB], z_t[:], brep[:]))
            for dc in range(DCH)][-1][-1], deps=[e_bc, e_st1, e_st2])
        e_y_all = []
        for m in range(DCH):
            ps = ps_a if m % 2 == 0 else ps_b
            e_mm = p.step("pe", lambda e, m=m, ps=ps: [
                e.matmul(ps[:], w_in_bf[:, k * D + m * 128:k * D + (m + 1) * 128],
                         xb[:, k * SLAB:(k + 1) * SLAB],
                         start=(k == 0), stop=(k == DCH - 1))
                for k in range(DCH)][-1],
                deps=[e_xn, ev_wcast[-1], last_xin_evac[m % 2]])
            e_ev = p.step("dve", lambda e, m=m, ps=ps: e.tensor_copy(
                xin_bf[:, m * SLAB:(m + 1) * SLAB], ps[:]), deps=[e_mm])
            e_y = p.step("dve", lambda e, m=m: e.scalar_tensor_tensor(
                xy[:, m * SLAB:(m + 1) * SLAB],
                xin_bf[:, m * SLAB:(m + 1) * SLAB], Dp_sc[:, m:m + 1],
                xy[:, m * SLAB:(m + 1) * SLAB], ALU.mult, ALU.add),
                deps=[e_ev])
            last_xin_evac[m % 2] = e_y
            e_y_all.append(e_y)
        e_y0d = p.dma("pool", [(y0_dram[dc * 128:(dc + 1) * 128, c0:c0 + SLAB],
                                xy[:, dc * SLAB:(dc + 1) * SLAB])
                               for dc in range(DCH)], deps=[e_y_all[-1]])
        e_y0_list.append(e_y0d)
        e_prev_y0 = e_y0d
        e_pdl = p.step("pe", lambda e: [
            e.matmul(ps_dl[:], wx_bf[:, k * 96:k * 96 + DTR],
                     xin_bf[:, k * SLAB:(k + 1) * SLAB],
                     start=(k == 0), stop=(k == DCH - 1))
            for k in range(DCH)][-1], deps=[e_y_all[-1], ev_wready])
        e_pB = p.step("pe", lambda e: [
            e.matmul(ps_B[:], wx_bf[:, k * 96 + DTR:k * 96 + DTR + NST],
                     xin_bf[:, k * SLAB:(k + 1) * SLAB],
                     start=(k == 0), stop=(k == DCH - 1))
            for k in range(DCH)][-1], deps=[e_pdl])
        e_pC = p.step("pe", lambda e: [
            e.matmul(ps_C[:], wx_bf[:, k * 96 + 80:k * 96 + 96],
                     xin_bf[:, k * SLAB:(k + 1) * SLAB],
                     start=(k == 0), stop=(k == DCH - 1))
            for k in range(DCH)][-1], deps=[e_pB])
        e_dlb = p.step("dve", lambda e: e.tensor_copy(dtl_bf[:], ps_dl[:]), deps=[e_pdl])
        e_Bsb = p.step("dve", lambda e: e.tensor_copy(B_sb[:], ps_B[:]), deps=[e_pB])
        e_Csb = p.step("dve", lambda e: e.tensor_copy(g_sb[:], ps_C[:]), deps=[e_pC])
        e_cdma = p.dma("pool", [(c_dram[:, c0:c0 + SLAB], g_sb[:])], deps=[e_Csb])
        e_dt_slab = []
        prev_sp2 = (None, 0)
        for m in range(DCH):
            ps = ps_a if m % 2 == 0 else ps_b
            e_mm = p.step("pe", lambda e, m=m, ps=ps: e.matmul(
                ps[:], wdt_bf[:, m * 128:(m + 1) * 128], dtl_bf[:],
                start=True, stop=True), deps=[e_dlb, last_xin_evac[m % 2]])
            e_sp1 = p.step("act", lambda e, m=m, ps=ps: (
                e.activation(r_t[:], ps[:], AF.Relu, bias=bdt_sc[:, m:m + 1]),
                e.activation(z_t[:], ps[:], AF.Abs, bias=bdt_sc[:, m:m + 1]),
                e.activation(z_t[:], z_t[:], AF.Exp, scale=cns[:, 2:3]),
                e.activation(z_t[:], z_t[:], AF.Ln, bias=cns[:, 3:4]))[-1],
                deps=[e_mm, e_rs, prev_sp2])
            e_sp2 = p.step("dve", lambda e, m=m, c0=c0: (
                e.tensor_add(r_t[:], r_t[:], z_t[:]),
                e.tensor_scalar(dt_all[:, m * T + c0: m * T + c0 + SLAB],
                                r_t[:], DT_MAX - DT_MIN, DT_MIN, ALU.mult, ALU.add),
                e.tensor_mul(sq[:, m * SLAB:(m + 1) * SLAB],
                             dt_all[:, m * T + c0: m * T + c0 + SLAB],
                             xin_bf[:, m * SLAB:(m + 1) * SLAB]))[-1],
                deps=[e_sp1])
            last_xin_evac[m % 2] = e_sp2
            prev_sp2 = e_sp2
            e_dt_slab.append(e_sp2)
        e_dt_done.append(e_dt_slab[-1])
        e_wg = p.step("pe", lambda e: [
            e.matmul(ps_g[:], A_bf[:, k * 16:(k + 1) * 16],
                     sq[:, k * SLAB:(k + 1) * SLAB],
                     start=(k == 0), stop=(k == DCH - 1))
            for k in range(DCH)][-1], deps=e_dt_slab + [e_A3])
        e_g = p.step("dve", lambda e: e.tensor_mul(g_sb[:], B_sb[:], ps_g[:]),
                     deps=[e_wg, e_cdma, e_Bsb])
        e_gd = p.dma("pool", [(g_dram[:, c0:c0 + SLAB], g_sb[:])], deps=[e_g])
        e_g_done.append(e_gd)
        fence = [e_wg, e_g, e_sp1, e_gd]

    e_p1 = e_g_done[-1]

    if stage < 4:
        p.dma("pool", [(dbg["dbg_dt"][:], dt_all[:])], deps=e_dt_done)
        p.dma("pool", [(dbg["dbg_g"][:], g_dram[:]), (dbg["dbg_c"][:], c_dram[:])],
              deps=e_g_done)
        p.dma("pool", [(dbg["dbg_u"][:], xin_bf[:])], deps=[e_p1])

    dt_ready = e_dt_done[-1]
    g_ready = e_g_done[-1]

    if stage >= 2:
        pfree("ps_g", "ps_C", "ps_B", "ps_dl", "ps_b", "ps_a", "ps_sq", "ps_su")
        free("wld", "r_t", "z_t", "g_sb", "B_sb", "dtl_bf", "brep", "arep",
             "bw_r", "vr_r", "mu_r", "xin_bf", "sq", "xb", "xy", "wdt_bf",
             "wx_bf", "w_in_bf")

        barAF = [alloc("barAF0", [128, T], F32), alloc("barAF1", [128, T], F32)]
        HF = [alloc("HF0", [128, T], F32), alloc("HF1", [128, T], F32)]
        grepF = alloc("grepF", [128, T], F32)
        ss_all = alloc("ss_all", [128, 64], F32)
        E_t = alloc("E_t", [128, 16], F32)
        rcv_sb = alloc("rcv_sb", [128, 8 * 136], F32)

        e_scan = [(None, 0), (None, 0)]
        e_cpA = {}
        e_cp_par = [(None, 0), (None, 0)]
        e_grepF = (None, 0)
        ui = 0
        for n in range(NST):
            e_grepF = p.dma("pool", [
                (grepF[:], g_dram[n:n + 1, :].broadcast_to([128, T]))],
                deps=[g_ready, e_scan[0], e_scan[1]])
            for dc in range(DCH):
                u = dc * 16 + n
                par = ui % 2
                e_bar = p.step("act", lambda e, dc=dc, u=u, par=par: e.activation(
                    barAF[par][:], dt_all[:, dc * T:(dc + 1) * T],
                    AF.Exp, scale=A_sc[:, u:u + 1]),
                    deps=[dt_ready, e_scan[par]])
                e_sc = p.step("dve", lambda e, par=par:
                              e.tensor_tensor_scan(HF[par][:], barAF[par][:],
                                                   grepF[:], 0.0,
                                                   ALU.mult, ALU.add),
                              deps=[e_bar, e_grepF, e_cp_par[par]])
                e_cpA[u] = p.step("pool", lambda e, par=par, u=u: e.tensor_copy(
                    Hb[:, u:u + 1], HF[par][:, T - 1:T]), deps=[e_sc])
                e_cp_par[par] = e_cpA[u]
                e_scan[par] = e_sc
                ui += 1

        e_sdt = p.step("dve", lambda e: [
            e.tensor_reduce(Sdt_sb[:, dc:dc + 1], dt_all[:, dc * T:(dc + 1) * T],
                            mybir.AxisListType.X, ALU.add)
            for dc in range(DCH)][-1], deps=[dt_ready, e_scan[0], e_scan[1]])
        e_snd = p.dma("pool", [(snd_d.ap()[:, 0:128], Hb[:]),
                               (snd_d.ap()[:, 128:136], Sdt_sb[:])],
                      deps=[e_sdt, e_scan[0], e_scan[1]])
        e_cc = p.step("pool", lambda e: e.collective_compute(
            "AllGather", ALU.bypass, replica_groups=[list(range(NCORES))],
            ins=[snd_d.ap().opt()], outs=[rcv_d.ap().opt()]),
            deps=[e_snd], sem="cc")
        e_rcv = p.dma("pool", [(rcv_sb[:, j * 136:(j + 1) * 136],
                                rcv_d.ap()[j * 128:(j + 1) * 128, :])
                               for j in range(NCORES)], deps=[e_cc])
        e_hin0 = p.step("dve", lambda e: e.memset(hin[:], 0.0), deps=[])
        e_ss = []
        for j in range(NCORES):
            e_c = p.step("dve", lambda e, j=j: e.tensor_scalar(
                ss_all[:, j * 8:(j + 1) * 8], ones8[:],
                m2b_sb[:, j * 9 + 8:j * 9 + 9], None, ALU.mult),
                deps=[e_rcv, e_ones8, e_cl])
            for i in range(NCORES):
                e_c = p.step("dve", lambda e, j=j, i=i: e.scalar_tensor_tensor(
                    ss_all[:, j * 8:(j + 1) * 8],
                    rcv_sb[:, i * 136 + 128:i * 136 + 136],
                    m2b_sb[:, j * 9 + i:j * 9 + i + 1],
                    ss_all[:, j * 8:(j + 1) * 8], ALU.mult, ALU.add),
                    deps=[e_c])
            e_ss.append(e_c)
        e_comb = e_hin0
        for j in range(NCORES):
            for dc in range(DCH):
                e_E = p.step("act", lambda e, j=j, dc=dc: e.activation(
                    E_t[:], A_sc[:, dc * 16:(dc + 1) * 16], AF.Exp,
                    scale=ss_all[:, j * 8 + dc:j * 8 + dc + 1]),
                    deps=[e_ss[j], e_comb])
                e_mu2 = p.step("dve", lambda e, j=j, dc=dc: e.tensor_mul(
                    E_t[:], E_t[:],
                    rcv_sb[:, j * 136 + dc * 16:j * 136 + (dc + 1) * 16]),
                    deps=[e_E, e_hin0])
                e_comb = p.step("dve", lambda e, dc=dc: e.tensor_add(
                    hin[:, dc * 16:(dc + 1) * 16],
                    hin[:, dc * 16:(dc + 1) * 16], E_t[:]),
                    deps=[e_mu2])
        e_hin = e_comb

        if stage < 4:
            p.dma("pool", [(dbg["dbg_hb"][:], Hb[:])], deps=[e_snd])
            p.dma("pool", [(dbg["dbg_hin"][:], hin[:])], deps=[e_hin])

    if stage >= 3:
        free("rcv_sb", "E_t", "ss_all", "grepF", "HF1", "HF0", "barAF1", "barAF0")

        Yq = alloc("Yq", [128, DCH * SLAB], F32)
        h1q = alloc("h1q", [128, MH * SLAB], BF16)
        h2q = alloc("h2q", [128, MH * SLAB], BF16)
        barQ = [alloc("barQ0", [128, SLAB], F32), alloc("barQ1", [128, SLAB], F32)]
        HQ = [alloc("HQ0", [128, SLAB], F32), alloc("HQ1", [128, SLAB], F32)]
        gq = alloc("gq", [128, SLAB], F32)
        cq = alloc("cq", [128, SLAB], F32)
        tmpq = alloc("tmpq", [128, SLAB], F32)
        xb2 = alloc("xb2", [128, DCH * SLAB], BF16)
        mu2_r = alloc("mu2_r", [1, SLAB], F32)
        vr2_r = alloc("vr2_r", [1, SLAB], F32)
        bw2_r = alloc("bw2_r", [1, SLAB], F32)
        ar2 = alloc("ar2", [128, SLAB], F32)
        br2 = alloc("br2", [128, SLAB], F32)
        zz2 = alloc("zz2", [128, SLAB], F32)
        stq = [alloc("stq0", [128, SLAB], BF16), alloc("stq1", [128, SLAB], BF16)]
        t2q = [alloc("t2q0", [128, SLAB], BF16), alloc("t2q1", [128, SLAB], BF16)]
        obq = [alloc("obq0", [128, SLAB], F32), alloc("obq1", [128, SLAB], F32)]
        xmq = [alloc("xmq0", [128, SLAB], F32), alloc("xmq1", [128, SLAB], F32)]
        ps2su = palloc("ps2su", [1, SLAB])
        ps2sq = palloc("ps2sq", [1, SLAB])
        ps_fa = [palloc("ps_fa0", [128, SLAB]), palloc("ps_fa1", [128, SLAB])]
        ps_fb = [palloc("ps_fb0", [128, SLAB]), palloc("ps_fb1", [128, SLAB])]

        st = type("S", (), {})()
        st.scanB = [(None, 0), (None, 0)]
        st.xm_list = []
        st.cpB = {}
        st.cpB_par = [(None, 0), (None, 0)]
        st.ymul = [(None, 0), (None, 0)]
        st.yq_prev = (None, 0)
        st.xm_prev = (None, 0)
        st.xn2 = (None, 0)
        st.l1mm_prev = {}
        st.l4_prev = (None, 0)
        st.prevbc2 = (None, 0)
        st.vr2_prev = (None, 0)
        st.wpar = {}
        st.evA = {}
        st.evB = {}
        st.out_total = None
        st.l1_done = []
        do_ffn = stage >= 4

        def passB_thunks(q):
            c0 = q * SLAB
            thunks = []

            def t_y0(q=q, c0=c0):
                st.y0in = p.dma("pool", [(Yq[:, dc * SLAB:(dc + 1) * SLAB],
                                          y0_dram[dc * 128:(dc + 1) * 128, c0:c0 + SLAB])
                                         for dc in range(DCH)],
                                deps=[st.yq_prev, e_y0_list[q], st.xm_prev])
            thunks.append(t_y0)
            for n in range(NST):
                def t_n(q=q, n=n, c0=c0):
                    dep_rep = [st.scanB[0], st.scanB[1], st.ymul[0], st.ymul[1], e_hin]
                    e_gq = p.dma("pool", [(gq[:], g_dram[n:n + 1, c0:c0 + SLAB]
                                           .broadcast_to([128, SLAB]))], deps=dep_rep)
                    e_cq = p.dma("pool", [(cq[:], c_dram[n:n + 1, c0:c0 + SLAB]
                                           .broadcast_to([128, SLAB]))], deps=dep_rep)
                    for dc in range(DCH):
                        u = dc * 16 + n
                        par = dc % 2
                        e_bar = p.step("act", lambda e, u=u, par=par, dc=dc, c0=c0:
                                       e.activation(barQ[par][:],
                                                    dt_all[:, dc * T + c0: dc * T + c0 + SLAB],
                                                    AF.Exp, scale=A_sc[:, u:u + 1]),
                                       deps=[st.scanB[par], e_hin])
                        init = hin[:, u:u + 1] if q == 0 else Hb[:, u:u + 1]
                        dep_sc = [e_bar, e_gq, e_cq, st.cpB_par[par], st.ymul[par]]
                        if q > 0:
                            dep_sc.append(st.cpB[u])
                        e_sc = p.step("dve", lambda e, par=par, init=init:
                                      e.tensor_tensor_scan(HQ[par][:], barQ[par][:],
                                                           gq[:], init,
                                                           ALU.mult, ALU.add),
                                      deps=dep_sc)
                        if q < NSL - 1:
                            st.cpB[u] = p.step("pool", lambda e, par=par, u=u:
                                               e.tensor_copy(Hb[:, u:u + 1],
                                                             HQ[par][:, SLAB - 1:SLAB]),
                                               deps=[e_sc])
                            st.cpB_par[par] = st.cpB[u]
                        e_ym = p.step("dve", lambda e, par=par, dc=dc: (
                            e.tensor_mul(tmpq[:], HQ[par][:], cq[:]),
                            e.tensor_add(Yq[:, dc * SLAB:(dc + 1) * SLAB],
                                         Yq[:, dc * SLAB:(dc + 1) * SLAB], tmpq[:]))[-1],
                            deps=[e_sc, st.y0in])
                        st.ymul[par] = e_ym
                        st.scanB[par] = e_sc
                thunks.append(t_n)
            return thunks

        def ln2_emit(q):
            c0 = q * SLAB
            Xfq = Xf2[q % 2]
            e_passBq = [st.ymul[0], st.ymul[1]]
            e_cast2 = p.step("dve", lambda e: e.tensor_copy(xb2[:], Yq[:]),
                             deps=e_passBq + [st.prevbc2])
            e_st1 = p.step("pe", lambda e: [
                e.matmul(ps2su[:], ones_b[:, 0:1], xb2[:, dc * SLAB:(dc + 1) * SLAB],
                         start=(dc == 0), stop=(dc == DCH - 1))
                for dc in range(DCH)][-1], deps=[e_cast2, st.vr2_prev])
            e_sqr2 = p.step("act", lambda e: e.activation(xb2[:], Yq[:], AF.Square),
                            deps=[e_st1] + e_passBq)
            e_st2 = p.step("pe", lambda e: [
                e.matmul(ps2sq[:], ones_b[:, 0:1], xb2[:, dc * SLAB:(dc + 1) * SLAB],
                         start=(dc == 0), stop=(dc == DCH - 1))
                for dc in range(DCH)][-1], deps=[e_sqr2])
            e_mu2 = p.step("dve", lambda e: e.tensor_scalar_mul(mu2_r[:], ps2su[:], 1.0 / D),
                           deps=[e_st1])
            e_vr2 = p.step("dve", lambda e: (
                e.tensor_mul(vr2_r[:], mu2_r[:], mu2_r[:]),
                e.scalar_tensor_tensor(vr2_r[:], ps2sq[:], 1.0 / D, vr2_r[:],
                                       ALU.mult, ALU.subtract))[-1],
                deps=[e_mu2, e_st2])
            st.vr2_prev = e_vr2
            e_rs2 = p.step("act", lambda e: (
                e.activation(vr2_r[:], vr2_r[:], AF.Ln, bias=cns[0:1, 0:1]),
                e.activation(vr2_r[:], vr2_r[:], AF.Exp, scale=cns[0:1, 1:2]))[-1],
                deps=[e_vr2])
            e_bw2 = p.step("dve", lambda e: e.scalar_tensor_tensor(
                bw2_r[:], mu2_r[:], -1.0, vr2_r[:], ALU.mult, ALU.mult), deps=[e_rs2])
            e_ro2 = p.dma("pool", [(lnrow_d[0:1, :], vr2_r[:]),
                                   (lnrow_d[1:2, :], bw2_r[:])], deps=[e_bw2])
            e_bc2 = p.dma("pool", [(ar2[:], lnrow_d[0:1, :].broadcast_to([128, SLAB])),
                                   (br2[:], lnrow_d[1:2, :].broadcast_to([128, SLAB]))],
                          deps=[e_ro2])
            st.prevbc2 = e_bc2
            e_xn2 = p.step("dve", lambda e, Xfq=Xfq: [
                (e.tensor_mul(zz2[:], Yq[:, dc * SLAB:(dc + 1) * SLAB], ar2[:]),
                 e.tensor_add(Xfq[:, dc * SLAB:(dc + 1) * SLAB], zz2[:], br2[:]))
                for dc in range(DCH)][-1][-1],
                deps=[e_bc2, e_st1, e_st2, st.l1mm_prev.get(q % 2, (None, 0))])
            e_xm = p.dma("pool", [(xmid_d[dc * 128:(dc + 1) * 128, c0:c0 + SLAB],
                                   Yq[:, dc * SLAB:(dc + 1) * SLAB])
                                  for dc in range(DCH)], deps=[e_xn2])
            st.xn2 = e_xn2
            st.yq_prev = e_xn2
            st.xm_prev = e_xm
            st.xm = e_xm
            st.xm_list.append(e_xm)

        def ffn_thunks(q):
            c0 = q * SLAB
            Xfq = Xf2[q % 2]
            thunks = []
            for m in range(MH):
                def t_l1(m=m, q=q, Xfq=Xfq):
                    wp = m % 2
                    pp = m % 2
                    e_w = p.dma("sp", [
                        (wcur[wp][:, 0:1024], w1m_h[m].rearrange("p k c -> p (k c)")),
                        (wcur[wp][:, 1024:2048], w2m_h[m].rearrange("p k c -> p (k c)"))],
                        deps=[st.wpar.get(wp, (None, 0))])
                    e_mA = p.step("pe", lambda e, wp=wp, pp=pp, Xfq=Xfq: [
                        e.matmul(ps_fa[pp][:], wcur[wp][:, k * 128:(k + 1) * 128],
                                 Xfq[:, k * SLAB:(k + 1) * SLAB],
                                 start=(k == 0), stop=(k == DCH - 1))
                        for k in range(DCH)][-1],
                        deps=[e_w, st.xn2, st.evA.get(pp, (None, 0))])
                    e_mB = p.step("pe", lambda e, wp=wp, pp=pp, Xfq=Xfq: [
                        e.matmul(ps_fb[pp][:],
                                 wcur[wp][:, 1024 + k * 128:1024 + (k + 1) * 128],
                                 Xfq[:, k * SLAB:(k + 1) * SLAB],
                                 start=(k == 0), stop=(k == DCH - 1))
                        for k in range(DCH)][-1],
                        deps=[e_w, st.evB.get(pp, (None, 0))])
                    e_s = p.step("act", lambda e, m=m, pp=pp: (
                        e.activation(stq[pp][:], ps_fa[pp][:], AF.Silu,
                                     scale=s1_sc[:, m:m + 1]),
                        e.activation(t2q[pp][:], ps_fb[pp][:], AF.Copy,
                                     scale=s2_sc[:, m:m + 1]))[-1],
                        deps=[e_mA, e_mB])
                    e_h = p.step("dve", lambda e, m=m, pp=pp: e.tensor_mul(
                        h1q[:, m * SLAB:(m + 1) * SLAB], stq[pp][:], t2q[pp][:]),
                        deps=[e_s, st.l4_prev])
                    st.evA[pp] = e_h
                    st.evB[pp] = e_h
                    st.wpar[wp] = e_mB
                    if m == MH - 1:
                        st.l1mm_prev[q % 2] = e_mB
                        st.l1_done = [st.evA[0], st.evA[1]]
                thunks.append(t_l1)
            for layer, (wm_h, sc_sb, src_t, dst_t) in enumerate([
                    (h1m_h, sh1_sc, h1q, h2q), (h2m_h, sh2_sc, h2q, h1q)]):
                for m in range(MH):
                    def t_l23(m=m, layer=layer, wm_h=wm_h, sc_sb=sc_sb,
                              src_t=src_t, dst_t=dst_t):
                        wp = m % 2
                        pp = m % 2
                        prev_dep = st.l1_done if layer == 0 and m < 2 else []
                        e_w = p.dma("sp", [(wcur[wp][:],
                                            wm_h[m].rearrange("p k c -> p (k c)"))],
                                    deps=[st.wpar.get(wp, (None, 0))])
                        e_mm = p.step("pe", lambda e, wp=wp, pp=pp, src_t=src_t: [
                            e.matmul(ps_fa[pp][:], wcur[wp][:, k * 128:(k + 1) * 128],
                                     src_t[:, k * SLAB:(k + 1) * SLAB],
                                     start=(k == 0), stop=(k == MH - 1))
                            for k in range(MH)][-1],
                            deps=[e_w, st.evA.get(pp, (None, 0))] + prev_dep)
                        e_g2 = p.step("act", lambda e, m=m, pp=pp, dst_t=dst_t,
                                      sc_sb=sc_sb:
                                      e.activation(dst_t[:, m * SLAB:(m + 1) * SLAB],
                                                   ps_fa[pp][:], AF.Gelu,
                                                   scale=sc_sb[:, m:m + 1]),
                                      deps=[e_mm])
                        st.evA[pp] = e_g2
                        st.wpar[wp] = e_mm
                    thunks.append(t_l23)
            for m in range(DCH):
                def t_l4(m=m, q=q, c0=c0):
                    wp = m % 2
                    pp = m % 2
                    e_w = p.dma("sp", [(wcur[wp][:],
                                        w3m_h[m].rearrange("p k c -> p (k c)"))],
                                deps=[st.wpar.get(wp, (None, 0))])
                    e_mm = p.step("pe", lambda e, wp=wp, pp=pp: [
                        e.matmul(ps_fa[pp][:], wcur[wp][:, k * 128:(k + 1) * 128],
                                 h1q[:, k * SLAB:(k + 1) * SLAB],
                                 start=(k == 0), stop=(k == MH - 1))
                        for k in range(MH)][-1],
                        deps=[e_w, st.evA.get(pp, (None, 0)),
                              st.evB.get((9, pp), (None, 0))])
                    e_o = p.step("act", lambda e, m=m, pp=pp: e.activation(
                        obq[pp][:], ps_fa[pp][:], AF.Copy, scale=s3_sc[:, m:m + 1]),
                        deps=[e_mm, st.evB.get((9, pp), (None, 0))])
                    st.evA[pp] = e_o
                    e_xmi = p.dma("pool", [(xmq[pp][:],
                                            xmid_d[m * 128:(m + 1) * 128, c0:c0 + SLAB])],
                                  deps=[st.xm, e_o, st.evB.get((8, pp), (None, 0))])
                    e_add = p.step("dve", lambda e, pp=pp: e.tensor_add(
                        obq[pp][:], obq[pp][:], xmq[pp][:]), deps=[e_o, e_xmi])
                    st.evB[(8, pp)] = e_add
                    e_od = p.dma("pool", [(out_h[m * 128:(m + 1) * 128, c0:c0 + SLAB],
                                           obq[pp][:])], deps=[e_add])
                    st.evB[(9, pp)] = e_od
                    st.out_total = e_od
                    st.wpar[wp] = e_mm
                    if m == DCH - 1:
                        st.l4_prev = e_mm
                thunks.append(t_l4)
            return thunks

        for t in passB_thunks(0):
            t()
        for q in range(NSL):
            ln2_emit(q)
            ft = ffn_thunks(q) if do_ffn else []
            bt = passB_thunks(q + 1) if q + 1 < NSL else []
            if not ft:
                for t in bt:
                    t()
                continue
            bi = 0
            nf, nb = len(ft), len(bt)
            for i, t in enumerate(ft):
                t()
                want = ((i + 1) * nb) // nf
                while bi < want:
                    bt[bi]()
                    bi += 1
            while bi < nb:
                bt[bi]()
                bi += 1
        e_out_total = st.out_total
        e_xm_list = st.xm_list

        if stage in (3, 5):
            p.dma("pool", [(dbg["dbg_xmid"][:], xmid_d[:])],
                  deps=e_xm_list)

    if stage >= 4:
        assert e_out_total is not None
        p.wait_only("pool", [(s, p.cnt[s]) for s in p.dma_pools["pool"]])
    else:
        p.wait_only("pool", [(s, p.cnt[s]) for s in p.dma_pools["pool"]])

    with nc.Block() as block:
        p.emit(block)

    return nc


def _shard_inputs(inputs):
    x = np.ascontiguousarray(inputs["L0_attn_x"])
    bf = ml_dtypes.bfloat16

    def tile_w(w, kk, mm):
        o, i = w.shape
        r = w.reshape(mm, 128, kk, 128)
        return np.ascontiguousarray(r.transpose(0, 3, 2, 1)).astype(bf)

    w1m = tile_w(np.asarray(inputs["L0_ffn_w1_w"], np.float32), 8, 32)
    w2m = tile_w(np.asarray(inputs["L0_ffn_w2_w"], np.float32), 8, 32)
    h1m = tile_w(np.asarray(inputs["L0_ffn_h1_w"], np.float32), 32, 32)
    h2m = tile_w(np.asarray(inputs["L0_ffn_h2_w"], np.float32), 32, 32)
    w3m = tile_w(np.asarray(inputs["L0_ffn_w3_w"], np.float32), 32, 8)

    def cols(v, k):
        return np.ascontiguousarray(np.asarray(v, np.float32).reshape(k, 128).T)

    common = dict(
        w_in=np.ascontiguousarray(np.asarray(inputs["L0_attn_w_in"], np.float32)),
        w_x=np.ascontiguousarray(np.asarray(inputs["L0_attn_w_x"], np.float32)),
        w_dt=np.ascontiguousarray(np.asarray(inputs["L0_attn_w_dt"], np.float32)),
        b_dt=cols(inputs["L0_attn_b_dt"], 8),
        A_log=np.ascontiguousarray(np.asarray(inputs["L0_attn_A_log"], np.float32)),
        Dp=cols(inputs["L0_attn_Dp"], 8),
        w1m=w1m, w2m=w2m, h1m=h1m, h2m=h2m, w3m=w3m,
        s1=cols(inputs["L0_ffn_w1_s"], 32),
        s2=cols(inputs["L0_ffn_w2_s"], 32),
        sh1=cols(inputs["L0_ffn_h1_s"], 32),
        sh2=cols(inputs["L0_ffn_h2_s"], 32),
        s3=cols(inputs["L0_ffn_w3_s"], 8),
        onesc=np.ones((128, 1), np.float32),
        cns=np.ascontiguousarray(np.broadcast_to(
            np.array([1e-5, -0.5, -1.0, 1.0], np.float32), (128, 4))),
    )

    in_maps = []
    for c in range(NCORES):
        bat, pos = c // 4, c % 4
        xs = np.ascontiguousarray(
            np.asarray(x[bat, pos * T:(pos + 1) * T, :], np.float32).T)
        m2b = np.zeros((8, 9), np.float32)
        for j in range(8):
            incl = (j // 4 == bat) and (j % 4 < pos)
            m2b[j, 8] = 0.0 if incl else BIG
            for i in range(8):
                m2b[j, i] = 1.0 if (incl and i // 4 == bat
                                    and j % 4 < i % 4 < pos) else 0.0
        m2b_t = np.ascontiguousarray(
            np.broadcast_to(m2b.reshape(1, 72), (128, 72)))
        im = dict(common)
        im["x"] = xs
        im["m2b"] = m2b_t
        in_maps.append(im)
    return in_maps


def kernel(**inputs):
    in_maps = _shard_inputs(inputs)
    nc = build_program(4)
    res = run_bass_kernel_spmd(nc, in_maps, core_ids=list(range(NCORES)))
    out = np.empty((2, 8192, 1024), np.float32)
    for c in range(NCORES):
        bat, pos = c // 4, c % 4
        out[bat, pos * T:(pos + 1) * T, :] = res.results[c]["out"].T
    return out


# version 08
# speedup vs baseline: 1.2049x; 1.0579x over previous
import numpy as np
import ml_dtypes
import concourse.bass as bass
import concourse.mybir as mybir
from concourse.bass_utils import run_bass_kernel_spmd

F32 = mybir.dt.float32
BF16 = mybir.dt.bfloat16
AF = mybir.ActivationFunctionType
ALU = mybir.AluOpType

NCORES = 8
D = 1024
T = 2048
NST = 16
DTR = 64
DCH = 8
HID = 4096
MH = HID // 128
SLAB = 512
NSL = T // SLAB
BIG = 1.0e30
DT_MIN, DT_MAX = 0.001, 0.1


class Prog:
    def __init__(self, nc):
        self.nc = nc
        self.events = []
        self.cnt = {}
        self.sems = {}
        self.guards = []

    def mksem(self, name):
        g = self.nc.semaphore(name)
        h = g.__enter__()
        self.guards.append(g)
        self.sems[name] = h
        self.cnt[name] = 0

    def step(self, eng, emit, deps=(), inc=1, sem=None):
        sname = sem or eng
        waits = [(self.sems[s], v) for (s, v) in deps if v > 0]
        if inc:
            self.cnt[sname] += inc
        after = self.cnt[sname]
        smh = self.sems[sname]

        def fn(e, waits=waits, inc=inc, smh=smh, emit=emit):
            for h, v in waits:
                e.wait_ge(h, v)
            inst = emit(e)
            if inc:
                inst.then_inc(smh, inc)

        self.events.append((eng, fn))
        return (sname, after)

    def dma(self, eng, transfers, deps=(), sem=None):
        if sem is None:
            pool = self.dma_pools[eng]
            sname = pool[self.dma_rr[eng] % len(pool)]
            self.dma_rr[eng] += 1
        else:
            sname = sem
        waits = [(self.sems[s], v) for (s, v) in deps if v > 0]
        prev_total = self.cnt[sname]
        self.cnt[sname] += 16 * len(transfers)
        after = self.cnt[sname]
        smh = self.sems[sname]

        def fn(e, waits=waits, transfers=transfers, smh=smh, prev=prev_total):
            if prev > 0:
                e.wait_ge(smh, prev)
            for h, v in waits:
                e.wait_ge(h, v)
            for out, in_ in transfers:
                e.dma_start(out=out, in_=in_).then_inc(smh, 16)

        self.events.append((eng, fn))
        return (sname, after)

    def wait_only(self, eng, deps):
        waits = [(self.sems[s], v) for (s, v) in deps if v > 0]

        def fn(e, waits=waits):
            for h, v in waits:
                e.wait_ge(h, v)

        self.events.append((eng, fn))

    def emit(self, block):
        evs = self.events

        @block.sync
        def _(e):
            for k, fn in evs:
                if k == "sp":
                    fn(e)

        @block.tensor
        def _(e):
            for k, fn in evs:
                if k == "pe":
                    fn(e)

        @block.scalar
        def _(e):
            for k, fn in evs:
                if k == "act":
                    fn(e)

        @block.vector
        def _(e):
            for k, fn in evs:
                if k == "dve":
                    fn(e)

        @block.gpsimd
        def _(e):
            for k, fn in evs:
                if k == "pool":
                    fn(e)


def build_program(stage=4):
    full = stage >= 4
    nc = bass.Bass("TRN2", target_bir_lowering=False, detect_race_conditions=False)

    def din(name, shape, dt=F32):
        return nc.dram_tensor(name, list(shape), dt, kind="ExternalInput").ap()

    x_h = din("x", [D, T])
    w_in_h = din("w_in", [D, D])
    w_x_h = din("w_x", [D, 96])
    w_dt_h = din("w_dt", [DTR, D])
    bdt_h = din("b_dt", [128, DCH])
    Alog_h = din("A_log", [D, NST])
    Dp_h = din("Dp", [128, DCH])
    w1m_h = din("w1m", [32, 128, 8, 128], BF16)
    w2m_h = din("w2m", [32, 128, 8, 128], BF16)
    h1m_h = din("h1m", [32, 128, 32, 128], BF16)
    h2m_h = din("h2m", [32, 128, 32, 128], BF16)
    w3m_h = din("w3m", [8, 128, 32, 128], BF16)
    s1_h = din("s1", [128, 32])
    s2_h = din("s2", [128, 32])
    sh1_h = din("sh1", [128, 32])
    sh2_h = din("sh2", [128, 32])
    s3_h = din("s3", [128, DCH])
    m2b_h = din("m2b", [128, 72])
    ones_h = din("onesc", [128, 1])
    cns_h = din("cns", [128, 4])

    out_h = nc.dram_tensor("out", [D, T], F32, kind="ExternalOutput").ap()
    dbg = {}
    if stage == 5:
        dbg["dbg_xmid"] = nc.dram_tensor("dbg_xmid", [D, T], F32, kind="ExternalOutput").ap()
    if stage < 4:
        def dout(name, shape, dt=F32):
            dbg[name] = nc.dram_tensor(name, list(shape), dt, kind="ExternalOutput").ap()
            return dbg[name]
        dout("dbg_dt", [128, DCH * T], BF16)
        dout("dbg_u", [128, DCH * SLAB], BF16)
        dout("dbg_g", [NST, T])
        dout("dbg_c", [NST, T])
        if stage >= 2:
            dout("dbg_hb", [128, 128])
            dout("dbg_hin", [128, 128])
        if stage >= 3:
            dout("dbg_xmid", [D, T])

    g_dram = nc.dram_tensor("g_scr", [NST, T], F32).ap()
    c_dram = nc.dram_tensor("c_scr", [NST, T], F32).ap()
    y0_dram = nc.dram_tensor("y0_scr", [D, T], F32).ap()
    snd_d = nc.dram_tensor("snd_scr", [128, 136], F32)
    rcv_d = nc.dram_tensor("rcv_scr", [128 * NCORES, 136], F32)
    xmid_d = nc.dram_tensor("xmid_scr", [D, T], F32).ap()
    lnrow_d = nc.dram_tensor("lnrow_scr", [2, SLAB], F32).ap()

    p = Prog(nc)
    for s in ["pe", "act", "dve", "pool", "sp", "cc"]:
        p.mksem(s)
    p.dma_pools = {"sp": [], "pool": []}
    p.dma_rr = {"sp": 0, "pool": 0}
    for i in range(6):
        nm = "spq%d" % i
        p.mksem(nm)
        p.dma_pools["sp"].append(nm)
    for i in range(8):
        nm = "plq%d" % i
        p.mksem(nm)
        p.dma_pools["pool"].append(nm)

    live = {}

    def alloc(name, shape, dt, side="left"):
        g = nc.sbuf_tensor(name, list(shape), dt, side=side)
        t = g.__enter__()
        live[name] = g
        return t

    def free(*names):
        for n in names:
            live.pop(n).__exit__(None, None, None)

    palloc_guards = {}

    def palloc(name, shape, dt=F32):
        g = nc.psum_tensor(name, list(shape), dt)
        t = g.__enter__()
        palloc_guards[name] = g
        return t

    def pfree(*names):
        for n in names:
            palloc_guards.pop(n).__exit__(None, None, None)

    A_sc = alloc("A_sc", [128, 128], F32)
    A_bf = alloc("A_bf", [128, 128], BF16)
    bdt_sc = alloc("bdt_sc", [128, DCH], F32)
    Dp_sc = alloc("Dp_sc", [128, DCH], F32)
    s1_sc = alloc("s1_sc", [128, 32], F32)
    s2_sc = alloc("s2_sc", [128, 32], F32)
    sh1_sc = alloc("sh1_sc", [128, 32], F32)
    sh2_sc = alloc("sh2_sc", [128, 32], F32)
    s3_sc = alloc("s3_sc", [128, DCH], F32)
    m2b_sb = alloc("m2b_sb", [128, 72], F32)
    ones_f = alloc("ones_f", [128, 1], F32)
    ones_b = alloc("ones_b", [128, 1], BF16)
    cns = alloc("cns_sb", [128, 4], F32)
    ones8 = alloc("ones8", [128, 8], F32)
    Hb = alloc("Hb", [128, 128], F32)
    hin = alloc("hin", [128, 128], F32)
    Sdt_sb = alloc("Sdt_sb", [128, DCH], F32)

    wcur = [alloc("wcur0", [128, 4096], BF16, side="right"),
            alloc("wcur1", [128, 4096], BF16, side="right")]
    Xf2 = [alloc("Xf0", [128, DCH * SLAB], BF16, side="right"),
           alloc("Xf1", [128, DCH * SLAB], BF16, side="right")]

    dt_all = alloc("dt_all", [128, DCH * T], BF16)

    e_cl = p.dma("sp", [
        (bdt_sc[:], bdt_h[:]), (Dp_sc[:], Dp_h[:]),
        (s1_sc[:], s1_h[:]), (s2_sc[:], s2_h[:]),
        (sh1_sc[:], sh1_h[:]), (sh2_sc[:], sh2_h[:]), (s3_sc[:], s3_h[:]),
        (m2b_sb[:], m2b_h[:]), (ones_f[:], ones_h[:]), (cns[:], cns_h[:]),
    ] + [(A_sc[:, dc * 16:(dc + 1) * 16], Alog_h[dc * 128:(dc + 1) * 128, :])
         for dc in range(DCH)])
    e_A1 = p.step("act", lambda e: e.activation(A_sc[:], A_sc[:], AF.Exp), deps=[e_cl])
    e_A2 = p.step("dve", lambda e: e.tensor_scalar_mul(A_sc[:], A_sc[:], -1.0), deps=[e_A1])
    e_A3 = p.step("dve", lambda e: e.tensor_copy(A_bf[:], A_sc[:]), deps=[e_A2])
    e_ones = p.step("dve", lambda e: e.tensor_copy(ones_b[:], ones_f[:]), deps=[e_cl])
    e_ones8 = p.step("dve", lambda e: e.memset(ones8[:], 1.0), deps=[])

    w_in_bf = alloc("w_in_bf", [128, DCH * D], BF16)
    wx_bf = alloc("wx_bf", [128, DCH * 96], BF16)
    wdt_bf = alloc("wdt_bf", [DTR, D], BF16)
    xy = alloc("xy", [128, DCH * SLAB], F32)
    xb = alloc("xb", [128, DCH * SLAB], BF16)
    sq = alloc("sq", [128, DCH * SLAB], BF16)
    xin_bf = alloc("xin_bf", [128, DCH * SLAB], BF16)
    mu_r = alloc("mu_r", [1, SLAB], F32)
    vr_r = alloc("vr_r", [1, SLAB], F32)
    bw_r = alloc("bw_r", [1, SLAB], F32)
    arep = alloc("arep", [128, SLAB], F32)
    brep = alloc("brep", [128, SLAB], F32)
    dtl_bf = alloc("dtl_bf", [DTR, SLAB], BF16)
    B_sb = alloc("B_sb", [NST, SLAB], F32)
    g_sb = alloc("g_sb", [NST, SLAB], F32)
    z_t = alloc("z_t", [128, SLAB], F32)
    r_t = alloc("r_t", [128, SLAB], F32)
    wld = alloc("wld", [128, D], F32)

    ev_wcast = []
    prev_cast = (None, 0)
    for k in range(DCH):
        dep = [] if k == 0 else [prev_cast]
        e_l = p.dma("sp", [(wld[:], w_in_h[k * 128:(k + 1) * 128, :])], deps=dep)
        prev_cast = p.step("dve", lambda e, k=k: e.tensor_copy(
            w_in_bf[:, k * D:(k + 1) * D], wld[:]), deps=[e_l])
        ev_wcast.append(prev_cast)
    for k in range(DCH):
        e_l = p.dma("sp", [(wld[:, 0:96], w_x_h[k * 128:(k + 1) * 128, :])], deps=[prev_cast])
        prev_cast = p.step("dve", lambda e, k=k: e.tensor_copy(
            wx_bf[:, k * 96:(k + 1) * 96], wld[:, 0:96]), deps=[e_l])
    e_l = p.dma("sp", [(wld[0:DTR, :], w_dt_h[:])], deps=[prev_cast])
    e_wdt = p.step("dve", lambda e: e.tensor_copy(wdt_bf[:], wld[0:DTR, :]), deps=[e_l])
    ev_wready = e_wdt

    ps_su = palloc("ps_su", [1, SLAB])
    ps_sq = palloc("ps_sq", [1, SLAB])
    ps_a = palloc("ps_a", [128, SLAB])
    ps_b = palloc("ps_b", [128, SLAB])
    ps_dl = palloc("ps_dl", [DTR, SLAB])
    ps_B = palloc("ps_B", [NST, SLAB])
    ps_C = palloc("ps_C", [NST, SLAB])
    ps_g = palloc("ps_g", [NST, SLAB])

    e_dt_done = []
    e_g_done = []
    e_y0_list = []
    e_prev_y0 = (None, 0)
    last_xin_evac = [(None, 0), (None, 0)]
    fence = []

    for s in range(NSL):
        c0 = s * SLAB

        e_x = p.dma("pool", [(xy[:, dc * SLAB:(dc + 1) * SLAB],
                              x_h[dc * 128:(dc + 1) * 128, c0:c0 + SLAB])
                             for dc in range(DCH)], deps=[e_prev_y0])
        e_cast = p.step("dve", lambda e: e.tensor_copy(xb[:], xy[:]),
                        deps=[e_x, e_ones] + fence)
        e_sqr = p.step("act", lambda e: e.activation(sq[:], xy[:], AF.Square),
                       deps=[e_x, e_A1] + fence)
        e_st1 = p.step("pe", lambda e: [
            e.matmul(ps_su[:], ones_b[:, 0:1], xb[:, dc * SLAB:(dc + 1) * SLAB],
                     start=(dc == 0), stop=(dc == DCH - 1))
            for dc in range(DCH)][-1], deps=[e_cast])
        e_st2 = p.step("pe", lambda e: [
            e.matmul(ps_sq[:], ones_b[:, 0:1], sq[:, dc * SLAB:(dc + 1) * SLAB],
                     start=(dc == 0), stop=(dc == DCH - 1))
            for dc in range(DCH)][-1], deps=[e_sqr])
        e_mu = p.step("dve", lambda e: e.tensor_scalar_mul(mu_r[:], ps_su[:], 1.0 / D),
                      deps=[e_st1])
        e_vr = p.step("dve", lambda e: (
            e.tensor_mul(vr_r[:], mu_r[:], mu_r[:]),
            e.scalar_tensor_tensor(vr_r[:], ps_sq[:], 1.0 / D, vr_r[:],
                                   ALU.mult, ALU.subtract))[-1],
            deps=[e_mu, e_st2])
        e_rs = p.step("act", lambda e: (
            e.activation(vr_r[:], vr_r[:], AF.Ln, bias=cns[0:1, 0:1]),
            e.activation(vr_r[:], vr_r[:], AF.Exp, scale=cns[0:1, 1:2]))[-1], deps=[e_vr])
        e_bw = p.step("dve", lambda e: e.scalar_tensor_tensor(
            bw_r[:], mu_r[:], -1.0, vr_r[:], ALU.mult, ALU.mult), deps=[e_rs])
        e_ro = p.dma("pool", [(lnrow_d[0:1, :], vr_r[:]), (lnrow_d[1:2, :], bw_r[:])],
                     deps=[e_bw])
        e_bc = p.dma("pool", [(arep[:], lnrow_d[0:1, :].broadcast_to([128, SLAB])),
                              (brep[:], lnrow_d[1:2, :].broadcast_to([128, SLAB]))],
                     deps=[e_ro])
        e_xn = p.step("dve", lambda e: [
            (e.tensor_mul(z_t[:], xy[:, dc * SLAB:(dc + 1) * SLAB], arep[:]),
             e.tensor_add(xb[:, dc * SLAB:(dc + 1) * SLAB], z_t[:], brep[:]))
            for dc in range(DCH)][-1][-1], deps=[e_bc, e_st1, e_st2])
        e_y_all = []
        for m in range(DCH):
            ps = ps_a if m % 2 == 0 else ps_b
            e_mm = p.step("pe", lambda e, m=m, ps=ps: [
                e.matmul(ps[:], w_in_bf[:, k * D + m * 128:k * D + (m + 1) * 128],
                         xb[:, k * SLAB:(k + 1) * SLAB],
                         start=(k == 0), stop=(k == DCH - 1))
                for k in range(DCH)][-1],
                deps=[e_xn, ev_wcast[-1], last_xin_evac[m % 2]])
            e_ev = p.step("dve", lambda e, m=m, ps=ps: e.tensor_copy(
                xin_bf[:, m * SLAB:(m + 1) * SLAB], ps[:]), deps=[e_mm])
            e_y = p.step("dve", lambda e, m=m: e.scalar_tensor_tensor(
                xy[:, m * SLAB:(m + 1) * SLAB],
                xin_bf[:, m * SLAB:(m + 1) * SLAB], Dp_sc[:, m:m + 1],
                xy[:, m * SLAB:(m + 1) * SLAB], ALU.mult, ALU.add),
                deps=[e_ev])
            last_xin_evac[m % 2] = e_y
            e_y_all.append(e_y)
        e_y0d = p.dma("pool", [(y0_dram[dc * 128:(dc + 1) * 128, c0:c0 + SLAB],
                                xy[:, dc * SLAB:(dc + 1) * SLAB])
                               for dc in range(DCH)], deps=[e_y_all[-1]])
        e_y0_list.append(e_y0d)
        e_prev_y0 = e_y0d
        e_pdl = p.step("pe", lambda e: [
            e.matmul(ps_dl[:], wx_bf[:, k * 96:k * 96 + DTR],
                     xin_bf[:, k * SLAB:(k + 1) * SLAB],
                     start=(k == 0), stop=(k == DCH - 1))
            for k in range(DCH)][-1], deps=[e_y_all[-1], ev_wready])
        e_pB = p.step("pe", lambda e: [
            e.matmul(ps_B[:], wx_bf[:, k * 96 + DTR:k * 96 + DTR + NST],
                     xin_bf[:, k * SLAB:(k + 1) * SLAB],
                     start=(k == 0), stop=(k == DCH - 1))
            for k in range(DCH)][-1], deps=[e_pdl])
        e_pC = p.step("pe", lambda e: [
            e.matmul(ps_C[:], wx_bf[:, k * 96 + 80:k * 96 + 96],
                     xin_bf[:, k * SLAB:(k + 1) * SLAB],
                     start=(k == 0), stop=(k == DCH - 1))
            for k in range(DCH)][-1], deps=[e_pB])
        e_dlb = p.step("dve", lambda e: e.tensor_copy(dtl_bf[:], ps_dl[:]), deps=[e_pdl])
        e_Bsb = p.step("dve", lambda e: e.tensor_copy(B_sb[:], ps_B[:]), deps=[e_pB])
        e_Csb = p.step("dve", lambda e: e.tensor_copy(g_sb[:], ps_C[:]), deps=[e_pC])
        e_cdma = p.dma("pool", [(c_dram[:, c0:c0 + SLAB], g_sb[:])], deps=[e_Csb])
        e_dt_slab = []
        prev_sp2 = (None, 0)
        for m in range(DCH):
            ps = ps_a if m % 2 == 0 else ps_b
            e_mm = p.step("pe", lambda e, m=m, ps=ps: e.matmul(
                ps[:], wdt_bf[:, m * 128:(m + 1) * 128], dtl_bf[:],
                start=True, stop=True), deps=[e_dlb, last_xin_evac[m % 2]])
            e_sp1 = p.step("act", lambda e, m=m, ps=ps: (
                e.activation(r_t[:], ps[:], AF.Relu, bias=bdt_sc[:, m:m + 1]),
                e.activation(z_t[:], ps[:], AF.Abs, bias=bdt_sc[:, m:m + 1]),
                e.activation(z_t[:], z_t[:], AF.Exp, scale=cns[:, 2:3]),
                e.activation(z_t[:], z_t[:], AF.Ln, bias=cns[:, 3:4]))[-1],
                deps=[e_mm, e_rs, prev_sp2])
            e_sp2 = p.step("dve", lambda e, m=m, c0=c0: (
                e.tensor_add(r_t[:], r_t[:], z_t[:]),
                e.tensor_scalar(dt_all[:, m * T + c0: m * T + c0 + SLAB],
                                r_t[:], DT_MAX - DT_MIN, DT_MIN, ALU.mult, ALU.add),
                e.tensor_mul(sq[:, m * SLAB:(m + 1) * SLAB],
                             dt_all[:, m * T + c0: m * T + c0 + SLAB],
                             xin_bf[:, m * SLAB:(m + 1) * SLAB]))[-1],
                deps=[e_sp1])
            last_xin_evac[m % 2] = e_sp2
            prev_sp2 = e_sp2
            e_dt_slab.append(e_sp2)
        e_dt_done.append(e_dt_slab[-1])
        e_wg = p.step("pe", lambda e: [
            e.matmul(ps_g[:], A_bf[:, k * 16:(k + 1) * 16],
                     sq[:, k * SLAB:(k + 1) * SLAB],
                     start=(k == 0), stop=(k == DCH - 1))
            for k in range(DCH)][-1], deps=e_dt_slab + [e_A3])
        e_g = p.step("dve", lambda e: e.tensor_mul(g_sb[:], B_sb[:], ps_g[:]),
                     deps=[e_wg, e_cdma, e_Bsb])
        e_gd = p.dma("pool", [(g_dram[:, c0:c0 + SLAB], g_sb[:])], deps=[e_g])
        e_g_done.append(e_gd)
        fence = [e_wg, e_g, e_sp1, e_gd]

    e_p1 = e_g_done[-1]

    if stage < 4:
        p.dma("pool", [(dbg["dbg_dt"][:], dt_all[:])], deps=e_dt_done)
        p.dma("pool", [(dbg["dbg_g"][:], g_dram[:]), (dbg["dbg_c"][:], c_dram[:])],
              deps=e_g_done)
        p.dma("pool", [(dbg["dbg_u"][:], xin_bf[:])], deps=[e_p1])

    dt_ready = e_dt_done[-1]
    g_ready = e_g_done[-1]

    if stage >= 2:
        pfree("ps_g", "ps_C", "ps_B", "ps_dl", "ps_b", "ps_a", "ps_sq", "ps_su")
        free("wld", "r_t", "z_t", "g_sb", "B_sb", "dtl_bf", "brep", "arep",
             "bw_r", "vr_r", "mu_r", "xin_bf", "sq", "xb", "xy", "wdt_bf",
             "wx_bf", "w_in_bf")

        barAF = [alloc("barAF0", [128, T], F32), alloc("barAF1", [128, T], F32)]
        HF = [alloc("HF0", [128, T], F32), alloc("HF1", [128, T], F32)]
        grepF = alloc("grepF", [128, T], F32)
        ss_all = alloc("ss_all", [128, 64], F32)
        E_t = alloc("E_t", [128, 16], F32)
        rcv_sb = alloc("rcv_sb", [128, 8 * 136], F32)

        e_scan = [(None, 0), (None, 0)]
        e_cpA = {}
        e_cp_par = [(None, 0), (None, 0)]
        e_grepF = (None, 0)
        ui = 0
        for n in range(NST):
            e_grepF = p.dma("pool", [
                (grepF[:], g_dram[n:n + 1, :].broadcast_to([128, T]))],
                deps=[g_ready, e_scan[0], e_scan[1]])
            for dc in range(DCH):
                u = dc * 16 + n
                par = ui % 2
                e_bar = p.step("act", lambda e, dc=dc, u=u, par=par: e.activation(
                    barAF[par][:], dt_all[:, dc * T:(dc + 1) * T],
                    AF.Exp, scale=A_sc[:, u:u + 1]),
                    deps=[dt_ready, e_scan[par]])
                e_sc = p.step("dve", lambda e, par=par:
                              e.tensor_tensor_scan(HF[par][:], barAF[par][:],
                                                   grepF[:], 0.0,
                                                   ALU.mult, ALU.add),
                              deps=[e_bar, e_grepF, e_cp_par[par]])
                e_cpA[u] = p.step("pool", lambda e, par=par, u=u: e.tensor_copy(
                    Hb[:, u:u + 1], HF[par][:, T - 1:T]), deps=[e_sc])
                e_cp_par[par] = e_cpA[u]
                e_scan[par] = e_sc
                ui += 1

        e_sdt = p.step("dve", lambda e: [
            e.tensor_reduce(Sdt_sb[:, dc:dc + 1], dt_all[:, dc * T:(dc + 1) * T],
                            mybir.AxisListType.X, ALU.add)
            for dc in range(DCH)][-1], deps=[dt_ready, e_scan[0], e_scan[1]])
        e_snd = p.dma("pool", [(snd_d.ap()[:, 0:128], Hb[:]),
                               (snd_d.ap()[:, 128:136], Sdt_sb[:])],
                      deps=[e_sdt, e_scan[0], e_scan[1]])
        e_cc = p.step("pool", lambda e: e.collective_compute(
            "AllGather", ALU.bypass, replica_groups=[list(range(NCORES))],
            ins=[snd_d.ap().opt()], outs=[rcv_d.ap().opt()]),
            deps=[e_snd], sem="cc")
        e_rcv = p.dma("pool", [(rcv_sb[:, j * 136:(j + 1) * 136],
                                rcv_d.ap()[j * 128:(j + 1) * 128, :])
                               for j in range(NCORES)], deps=[e_cc])
        e_hin0 = p.step("dve", lambda e: e.memset(hin[:], 0.0), deps=[])
        e_ss = []
        for j in range(NCORES):
            e_c = p.step("dve", lambda e, j=j: e.tensor_scalar(
                ss_all[:, j * 8:(j + 1) * 8], ones8[:],
                m2b_sb[:, j * 9 + 8:j * 9 + 9], None, ALU.mult),
                deps=[e_rcv, e_ones8, e_cl])
            for i in range(NCORES):
                e_c = p.step("dve", lambda e, j=j, i=i: e.scalar_tensor_tensor(
                    ss_all[:, j * 8:(j + 1) * 8],
                    rcv_sb[:, i * 136 + 128:i * 136 + 136],
                    m2b_sb[:, j * 9 + i:j * 9 + i + 1],
                    ss_all[:, j * 8:(j + 1) * 8], ALU.mult, ALU.add),
                    deps=[e_c])
            e_ss.append(e_c)
        e_comb = e_hin0
        for j in range(NCORES):
            for dc in range(DCH):
                e_E = p.step("act", lambda e, j=j, dc=dc: e.activation(
                    E_t[:], A_sc[:, dc * 16:(dc + 1) * 16], AF.Exp,
                    scale=ss_all[:, j * 8 + dc:j * 8 + dc + 1]),
                    deps=[e_ss[j], e_comb])
                e_mu2 = p.step("dve", lambda e, j=j, dc=dc: e.tensor_mul(
                    E_t[:], E_t[:],
                    rcv_sb[:, j * 136 + dc * 16:j * 136 + (dc + 1) * 16]),
                    deps=[e_E, e_hin0])
                e_comb = p.step("dve", lambda e, dc=dc: e.tensor_add(
                    hin[:, dc * 16:(dc + 1) * 16],
                    hin[:, dc * 16:(dc + 1) * 16], E_t[:]),
                    deps=[e_mu2])
        e_hin = e_comb

        if stage < 4:
            p.dma("pool", [(dbg["dbg_hb"][:], Hb[:])], deps=[e_snd])
            p.dma("pool", [(dbg["dbg_hin"][:], hin[:])], deps=[e_hin])

    if stage >= 3:
        free("rcv_sb", "E_t", "ss_all", "grepF", "HF1", "HF0", "barAF1", "barAF0")

        Yq = alloc("Yq", [128, DCH * SLAB], F32)
        h1q = alloc("h1q", [128, MH * SLAB], BF16)
        h2q = alloc("h2q", [128, MH * SLAB], BF16)
        barQ = [alloc("barQ0", [128, SLAB], F32), alloc("barQ1", [128, SLAB], F32)]
        HQ = [alloc("HQ0", [128, SLAB], F32), alloc("HQ1", [128, SLAB], F32)]
        gq = alloc("gq", [128, SLAB], F32)
        cq = alloc("cq", [128, SLAB], F32)
        tmpq = alloc("tmpq", [128, SLAB], F32)
        xb2 = alloc("xb2", [128, DCH * SLAB], BF16)
        mu2_r = alloc("mu2_r", [1, SLAB], F32)
        vr2_r = alloc("vr2_r", [1, SLAB], F32)
        bw2_r = alloc("bw2_r", [1, SLAB], F32)
        ar2 = alloc("ar2", [128, SLAB], F32)
        br2 = alloc("br2", [128, SLAB], F32)
        zz2 = alloc("zz2", [128, SLAB], F32)
        stq = [alloc("stq0", [128, SLAB], BF16), alloc("stq1", [128, SLAB], BF16)]
        t2q = [alloc("t2q0", [128, SLAB], BF16), alloc("t2q1", [128, SLAB], BF16)]
        obq = [alloc("obq%d" % i, [128, SLAB], F32) for i in range(4)]
        xmq = [alloc("xmq%d" % i, [128, SLAB], F32) for i in range(4)]
        ps2su = palloc("ps2su", [1, SLAB])
        ps2sq = palloc("ps2sq", [1, SLAB])
        ps_fa = [palloc("ps_fa0", [128, SLAB]), palloc("ps_fa1", [128, SLAB]),
                 palloc("ps_fa2", [128, SLAB]), palloc("ps_fa3", [128, SLAB])]
        ps_fb = [palloc("ps_fb0", [128, SLAB]), palloc("ps_fb1", [128, SLAB])]

        st = type("S", (), {})()
        st.scanB = [(None, 0), (None, 0)]
        st.xm_list = []
        st.cpB = {}
        st.cpB_par = [(None, 0), (None, 0)]
        st.ymul = [(None, 0), (None, 0)]
        st.yq_prev = (None, 0)
        st.xm_prev = (None, 0)
        st.xn2 = (None, 0)
        st.l1mm_prev = {}
        st.l4_prev = (None, 0)
        st.prevbc2 = (None, 0)
        st.vr2_prev = (None, 0)
        st.wpar = {}
        st.evA = {}
        st.evB = {}
        st.out_total = None
        st.l1_done = []
        do_ffn = stage >= 4

        def passB_thunks(q):
            c0 = q * SLAB
            thunks = []

            def t_y0(q=q, c0=c0):
                st.y0in = p.dma("pool", [(Yq[:, dc * SLAB:(dc + 1) * SLAB],
                                          y0_dram[dc * 128:(dc + 1) * 128, c0:c0 + SLAB])
                                         for dc in range(DCH)],
                                deps=[st.yq_prev, e_y0_list[q], st.xm_prev])
            thunks.append(t_y0)
            for n in range(NST):
                def t_n(q=q, n=n, c0=c0):
                    dep_rep = [st.scanB[0], st.scanB[1], st.ymul[0], st.ymul[1], e_hin]
                    e_gq = p.dma("pool", [(gq[:], g_dram[n:n + 1, c0:c0 + SLAB]
                                           .broadcast_to([128, SLAB]))], deps=dep_rep)
                    e_cq = p.dma("pool", [(cq[:], c_dram[n:n + 1, c0:c0 + SLAB]
                                           .broadcast_to([128, SLAB]))], deps=dep_rep)
                    for dc in range(DCH):
                        u = dc * 16 + n
                        par = dc % 2
                        e_bar = p.step("act", lambda e, u=u, par=par, dc=dc, c0=c0:
                                       e.activation(barQ[par][:],
                                                    dt_all[:, dc * T + c0: dc * T + c0 + SLAB],
                                                    AF.Exp, scale=A_sc[:, u:u + 1]),
                                       deps=[st.scanB[par], e_hin])
                        init = hin[:, u:u + 1] if q == 0 else Hb[:, u:u + 1]
                        dep_sc = [e_bar, e_gq, e_cq, st.cpB_par[par], st.ymul[par]]
                        if q > 0:
                            dep_sc.append(st.cpB[u])
                        e_sc = p.step("dve", lambda e, par=par, init=init:
                                      e.tensor_tensor_scan(HQ[par][:], barQ[par][:],
                                                           gq[:], init,
                                                           ALU.mult, ALU.add),
                                      deps=dep_sc)
                        if q < NSL - 1:
                            st.cpB[u] = p.step("pool", lambda e, par=par, u=u:
                                               e.tensor_copy(Hb[:, u:u + 1],
                                                             HQ[par][:, SLAB - 1:SLAB]),
                                               deps=[e_sc])
                            st.cpB_par[par] = st.cpB[u]
                        e_ym = p.step("dve", lambda e, par=par, dc=dc: (
                            e.tensor_mul(tmpq[:], HQ[par][:], cq[:]),
                            e.tensor_add(Yq[:, dc * SLAB:(dc + 1) * SLAB],
                                         Yq[:, dc * SLAB:(dc + 1) * SLAB], tmpq[:]))[-1],
                            deps=[e_sc, st.y0in])
                        st.ymul[par] = e_ym
                        st.scanB[par] = e_sc
                thunks.append(t_n)
            return thunks

        def ln2_emit(q):
            c0 = q * SLAB
            Xfq = Xf2[q % 2]
            e_passBq = [st.ymul[0], st.ymul[1]]
            e_cast2 = p.step("dve", lambda e: e.tensor_copy(xb2[:], Yq[:]),
                             deps=e_passBq + [st.prevbc2])
            e_st1 = p.step("pe", lambda e: [
                e.matmul(ps2su[:], ones_b[:, 0:1], xb2[:, dc * SLAB:(dc + 1) * SLAB],
                         start=(dc == 0), stop=(dc == DCH - 1))
                for dc in range(DCH)][-1], deps=[e_cast2, st.vr2_prev])
            e_sqr2 = p.step("act", lambda e: e.activation(xb2[:], Yq[:], AF.Square),
                            deps=[e_st1] + e_passBq)
            e_st2 = p.step("pe", lambda e: [
                e.matmul(ps2sq[:], ones_b[:, 0:1], xb2[:, dc * SLAB:(dc + 1) * SLAB],
                         start=(dc == 0), stop=(dc == DCH - 1))
                for dc in range(DCH)][-1], deps=[e_sqr2])
            e_mu2 = p.step("dve", lambda e: e.tensor_scalar_mul(mu2_r[:], ps2su[:], 1.0 / D),
                           deps=[e_st1])
            e_vr2 = p.step("dve", lambda e: (
                e.tensor_mul(vr2_r[:], mu2_r[:], mu2_r[:]),
                e.scalar_tensor_tensor(vr2_r[:], ps2sq[:], 1.0 / D, vr2_r[:],
                                       ALU.mult, ALU.subtract))[-1],
                deps=[e_mu2, e_st2])
            st.vr2_prev = e_vr2
            e_rs2 = p.step("act", lambda e: (
                e.activation(vr2_r[:], vr2_r[:], AF.Ln, bias=cns[0:1, 0:1]),
                e.activation(vr2_r[:], vr2_r[:], AF.Exp, scale=cns[0:1, 1:2]))[-1],
                deps=[e_vr2])
            e_bw2 = p.step("dve", lambda e: e.scalar_tensor_tensor(
                bw2_r[:], mu2_r[:], -1.0, vr2_r[:], ALU.mult, ALU.mult), deps=[e_rs2])
            e_ro2 = p.dma("pool", [(lnrow_d[0:1, :], vr2_r[:]),
                                   (lnrow_d[1:2, :], bw2_r[:])], deps=[e_bw2])
            e_bc2 = p.dma("pool", [(ar2[:], lnrow_d[0:1, :].broadcast_to([128, SLAB])),
                                   (br2[:], lnrow_d[1:2, :].broadcast_to([128, SLAB]))],
                          deps=[e_ro2])
            st.prevbc2 = e_bc2
            e_xn2 = p.step("dve", lambda e, Xfq=Xfq: [
                (e.tensor_mul(zz2[:], Yq[:, dc * SLAB:(dc + 1) * SLAB], ar2[:]),
                 e.tensor_add(Xfq[:, dc * SLAB:(dc + 1) * SLAB], zz2[:], br2[:]))
                for dc in range(DCH)][-1][-1],
                deps=[e_bc2, e_st1, e_st2, st.l1mm_prev.get(q % 2, (None, 0))])
            e_xm = p.dma("pool", [(xmid_d[dc * 128:(dc + 1) * 128, c0:c0 + SLAB],
                                   Yq[:, dc * SLAB:(dc + 1) * SLAB])
                                  for dc in range(DCH)], deps=[e_xn2])
            st.xn2 = e_xn2
            st.yq_prev = e_xn2
            st.xm_prev = e_xm
            st.xm = e_xm
            st.xm_list.append(e_xm)

        def ffn_thunks(q):
            c0 = q * SLAB
            Xfq = Xf2[q % 2]
            thunks = []
            for m in range(MH):
                def t_l1(m=m, q=q, Xfq=Xfq):
                    wp = m % 2
                    pp = m % 2
                    e_w = p.dma("sp", [
                        (wcur[wp][:, 0:1024], w1m_h[m].rearrange("p k c -> p (k c)")),
                        (wcur[wp][:, 1024:2048], w2m_h[m].rearrange("p k c -> p (k c)"))],
                        deps=[st.wpar.get(wp, (None, 0))])
                    e_mA = p.step("pe", lambda e, wp=wp, pp=pp, Xfq=Xfq: [
                        e.matmul(ps_fa[pp][:], wcur[wp][:, k * 128:(k + 1) * 128],
                                 Xfq[:, k * SLAB:(k + 1) * SLAB],
                                 start=(k == 0), stop=(k == DCH - 1))
                        for k in range(DCH)][-1],
                        deps=[e_w, st.xn2, st.evA.get(pp, (None, 0))])
                    e_mB = p.step("pe", lambda e, wp=wp, pp=pp, Xfq=Xfq: [
                        e.matmul(ps_fb[pp][:],
                                 wcur[wp][:, 1024 + k * 128:1024 + (k + 1) * 128],
                                 Xfq[:, k * SLAB:(k + 1) * SLAB],
                                 start=(k == 0), stop=(k == DCH - 1))
                        for k in range(DCH)][-1],
                        deps=[e_w, st.evB.get(pp, (None, 0))])
                    e_s = p.step("act", lambda e, m=m, pp=pp: (
                        e.activation(stq[pp][:], ps_fa[pp][:], AF.Silu,
                                     scale=s1_sc[:, m:m + 1]),
                        e.activation(t2q[pp][:], ps_fb[pp][:], AF.Copy,
                                     scale=s2_sc[:, m:m + 1]))[-1],
                        deps=[e_mA, e_mB])
                    e_h = p.step("dve", lambda e, m=m, pp=pp: e.tensor_mul(
                        h1q[:, m * SLAB:(m + 1) * SLAB], stq[pp][:], t2q[pp][:]),
                        deps=[e_s, st.l4_prev])
                    st.evA[pp] = e_h
                    st.evB[pp] = e_h
                    st.wpar[wp] = e_mB
                    if m == MH - 1:
                        st.l1mm_prev[q % 2] = e_mB
                        st.l1_done = [st.evA[0], st.evA[1]]
                thunks.append(t_l1)
            for layer, (wm_h, sc_sb, src_t, dst_t) in enumerate([
                    (h1m_h, sh1_sc, h1q, h2q), (h2m_h, sh2_sc, h2q, h1q)]):
                for m in range(MH):
                    def t_l23(m=m, layer=layer, wm_h=wm_h, sc_sb=sc_sb,
                              src_t=src_t, dst_t=dst_t):
                        wp = m % 2
                        pp = m % 4
                        prev_dep = st.l1_done if layer == 0 and m < 2 else []
                        e_w = p.dma("sp", [(wcur[wp][:],
                                            wm_h[m].rearrange("p k c -> p (k c)"))],
                                    deps=[st.wpar.get(wp, (None, 0))])
                        e_mm = p.step("pe", lambda e, wp=wp, pp=pp, src_t=src_t: [
                            e.matmul(ps_fa[pp][:], wcur[wp][:, k * 128:(k + 1) * 128],
                                     src_t[:, k * SLAB:(k + 1) * SLAB],
                                     start=(k == 0), stop=(k == MH - 1))
                            for k in range(MH)][-1],
                            deps=[e_w, st.evA.get(pp, (None, 0))] + prev_dep)
                        e_g2 = p.step("act", lambda e, m=m, pp=pp, dst_t=dst_t,
                                      sc_sb=sc_sb:
                                      e.activation(dst_t[:, m * SLAB:(m + 1) * SLAB],
                                                   ps_fa[pp][:], AF.Gelu,
                                                   scale=sc_sb[:, m:m + 1]),
                                      deps=[e_mm])
                        st.evA[pp] = e_g2
                        st.wpar[wp] = e_mm
                    thunks.append(t_l23)
            for m in range(DCH):
                def t_l4(m=m, q=q, c0=c0):
                    wp = m % 2
                    pp = m % 4
                    e_w = p.dma("sp", [(wcur[wp][:],
                                        w3m_h[m].rearrange("p k c -> p (k c)"))],
                                deps=[st.wpar.get(wp, (None, 0))])
                    e_mm = p.step("pe", lambda e, wp=wp, pp=pp: [
                        e.matmul(ps_fa[pp][:], wcur[wp][:, k * 128:(k + 1) * 128],
                                 h1q[:, k * SLAB:(k + 1) * SLAB],
                                 start=(k == 0), stop=(k == MH - 1))
                        for k in range(MH)][-1],
                        deps=[e_w, st.evA.get(pp, (None, 0)),
                              st.evB.get((9, pp), (None, 0))])
                    e_o = p.step("act", lambda e, m=m, pp=pp: e.activation(
                        obq[pp][:], ps_fa[pp][:], AF.Copy, scale=s3_sc[:, m:m + 1]),
                        deps=[e_mm, st.evB.get((9, pp), (None, 0))])
                    st.evA[pp] = e_o
                    e_xmi = p.dma("pool", [(xmq[pp][:],
                                            xmid_d[m * 128:(m + 1) * 128, c0:c0 + SLAB])],
                                  deps=[st.xm, e_o, st.evB.get((8, pp), (None, 0))])
                    e_add = p.step("dve", lambda e, pp=pp: e.tensor_add(
                        obq[pp][:], obq[pp][:], xmq[pp][:]), deps=[e_o, e_xmi])
                    st.evB[(8, pp)] = e_add
                    e_od = p.dma("pool", [(out_h[m * 128:(m + 1) * 128, c0:c0 + SLAB],
                                           obq[pp][:])], deps=[e_add])
                    st.evB[(9, pp)] = e_od
                    st.out_total = e_od
                    st.wpar[wp] = e_mm
                    if m == DCH - 1:
                        st.l4_prev = e_mm
                thunks.append(t_l4)
            return thunks

        for t in passB_thunks(0):
            t()
        for q in range(NSL):
            ln2_emit(q)
            ft = ffn_thunks(q) if do_ffn else []
            bt = passB_thunks(q + 1) if q + 1 < NSL else []
            if not ft:
                for t in bt:
                    t()
                continue
            bi = 0
            nf, nb = len(ft), len(bt)
            for i, t in enumerate(ft):
                t()
                want = ((i + 1) * nb) // nf
                while bi < want:
                    bt[bi]()
                    bi += 1
            while bi < nb:
                bt[bi]()
                bi += 1
        e_out_total = st.out_total
        e_xm_list = st.xm_list

        if stage in (3, 5):
            p.dma("pool", [(dbg["dbg_xmid"][:], xmid_d[:])],
                  deps=e_xm_list)

    if stage >= 4:
        assert e_out_total is not None
        p.wait_only("pool", [(s, p.cnt[s]) for s in p.dma_pools["pool"]])
    else:
        p.wait_only("pool", [(s, p.cnt[s]) for s in p.dma_pools["pool"]])

    with nc.Block() as block:
        p.emit(block)

    return nc


def _shard_inputs(inputs):
    x = np.ascontiguousarray(inputs["L0_attn_x"])
    bf = ml_dtypes.bfloat16

    def tile_w(w, kk, mm):
        o, i = w.shape
        r = w.reshape(mm, 128, kk, 128)
        return np.ascontiguousarray(r.transpose(0, 3, 2, 1)).astype(bf)

    w1m = tile_w(np.asarray(inputs["L0_ffn_w1_w"], np.float32), 8, 32)
    w2m = tile_w(np.asarray(inputs["L0_ffn_w2_w"], np.float32), 8, 32)
    h1m = tile_w(np.asarray(inputs["L0_ffn_h1_w"], np.float32), 32, 32)
    h2m = tile_w(np.asarray(inputs["L0_ffn_h2_w"], np.float32), 32, 32)
    w3m = tile_w(np.asarray(inputs["L0_ffn_w3_w"], np.float32), 32, 8)

    def cols(v, k):
        return np.ascontiguousarray(np.asarray(v, np.float32).reshape(k, 128).T)

    common = dict(
        w_in=np.ascontiguousarray(np.asarray(inputs["L0_attn_w_in"], np.float32)),
        w_x=np.ascontiguousarray(np.asarray(inputs["L0_attn_w_x"], np.float32)),
        w_dt=np.ascontiguousarray(np.asarray(inputs["L0_attn_w_dt"], np.float32)),
        b_dt=cols(inputs["L0_attn_b_dt"], 8),
        A_log=np.ascontiguousarray(np.asarray(inputs["L0_attn_A_log"], np.float32)),
        Dp=cols(inputs["L0_attn_Dp"], 8),
        w1m=w1m, w2m=w2m, h1m=h1m, h2m=h2m, w3m=w3m,
        s1=cols(inputs["L0_ffn_w1_s"], 32),
        s2=cols(inputs["L0_ffn_w2_s"], 32),
        sh1=cols(inputs["L0_ffn_h1_s"], 32),
        sh2=cols(inputs["L0_ffn_h2_s"], 32),
        s3=cols(inputs["L0_ffn_w3_s"], 8),
        onesc=np.ones((128, 1), np.float32),
        cns=np.ascontiguousarray(np.broadcast_to(
            np.array([1e-5, -0.5, -1.0, 1.0], np.float32), (128, 4))),
    )

    in_maps = []
    for c in range(NCORES):
        bat, pos = c // 4, c % 4
        xs = np.ascontiguousarray(
            np.asarray(x[bat, pos * T:(pos + 1) * T, :], np.float32).T)
        m2b = np.zeros((8, 9), np.float32)
        for j in range(8):
            incl = (j // 4 == bat) and (j % 4 < pos)
            m2b[j, 8] = 0.0 if incl else BIG
            for i in range(8):
                m2b[j, i] = 1.0 if (incl and i // 4 == bat
                                    and j % 4 < i % 4 < pos) else 0.0
        m2b_t = np.ascontiguousarray(
            np.broadcast_to(m2b.reshape(1, 72), (128, 72)))
        im = dict(common)
        im["x"] = xs
        im["m2b"] = m2b_t
        in_maps.append(im)
    return in_maps


def kernel(**inputs):
    in_maps = _shard_inputs(inputs)
    nc = build_program(4)
    res = run_bass_kernel_spmd(nc, in_maps, core_ids=list(range(NCORES)))
    out = np.empty((2, 8192, 1024), np.float32)
    for c in range(NCORES):
        bat, pos = c // 4, c % 4
        out[bat, pos * T:(pos + 1) * T, :] = res.results[c]["out"].T
    return out
